# version 04 (final)
# baseline (speedup 1.0000x reference)
import numpy as np
import concourse.bass as bass
import concourse.mybir as mybir
from concourse.bass_utils import run_bass_kernel_spmd
from concourse.alu_op_type import AluOpType as OP

F32 = mybir.dt.float32
I32 = mybir.dt.int32
BF16 = mybir.dt.bfloat16
AF = mybir.ActivationFunctionType
AX = mybir.AxisListType.X

M = 2048
D = 1024
NJ = 8
NCH = 4
CSZ = 512

DEBUG_UPTO = 0
DEBUG_DUMP = None
DEBUG_BASE = 0


class Prog:
    def __init__(self):
        self.nc = bass.Bass("TRN2", target_bir_lowering=False,
                            detect_race_conditions=False,
                            dynamic_dma_scratch_size=16384)
        self.sems = {}
        self.cnt = {}

    def sem(self, name):
        if name not in self.sems:
            self.sems[name] = self.nc.alloc_semaphore(name)
            self.cnt[name] = 0
        return self.sems[name]

    def inc(self, instr, name, k=1):
        instr.then_inc(self.sem(name), k)
        self.cnt[name] += k
        return instr

    def v(self, name):
        self.sem(name)
        return self.cnt[name]


def build():
    p = Prog()
    nc = p.nc
    dt = nc.dram_tensor

    x_in = dt("x_in", [D, M], F32, kind="ExternalInput").ap()
    wr_t = dt("wr_t", [D, D], F32, kind="ExternalInput").ap()
    wf_t = dt("wf_t", [D, D], F32, kind="ExternalInput").ap()
    attr_c = dt("attr_c", [128, 1], F32, kind="ExternalInput").ap()
    g0c = dt("g0c", [128, NJ], F32, kind="ExternalInput").ap()
    b0c = dt("b0c", [128, NJ], F32, kind="ExternalInput").ap()
    ones_r = dt("ones_r", [1, 128], F32, kind="ExternalInput").ap()
    ones_sq = dt("ones_sq", [128, 128], F32, kind="ExternalInput").ap()
    mask_e = dt("mask_e", [128, 1], F32, kind="ExternalInput").ap()
    mask_o = dt("mask_o", [128, 1], F32, kind="ExternalInput").ap()
    constc = dt("constc", [128, 6], F32, kind="ExternalInput").ap()
    clipx = dt("clipx", [1, 2], F32, kind="ExternalInput").ap()
    wg_n = dt("wg_n", [2816, D], F32, kind="ExternalInput").ap()
    wu_n = dt("wu_n", [2816, D], F32, kind="ExternalInput").ap()
    wd_n = dt("wd_n", [D, 2816], F32, kind="ExternalInput").ap()
    identb = dt("identb", [128, 128], BF16, kind="ExternalInput").ap()
    w_in_t = dt("w_in_t", [D, 2048], F32, kind="ExternalInput").ap()
    w_dt_t = dt("w_dt_t", [D, D], F32, kind="ExternalInput").ap()
    w_x_t = dt("w_x_t", [D, 32], F32, kind="ExternalInput").ap()
    w_out_t = dt("w_out_t", [D, D], F32, kind="ExternalInput").ap()
    a_log_c = dt("a_log_c", [128, 128], F32, kind="ExternalInput").ap()
    d_colh = dt("d_colh", [128, NJ], F32, kind="ExternalInput").ap()
    bdt_colh = dt("bdt_colh", [128, NJ], F32, kind="ExternalInput").ap()
    cw_colh = dt("cw_colh", [128, 24], F32, kind="ExternalInput").ap()
    g3ch = dt("g3ch", [128, NJ], F32, kind="ExternalInput").ap()
    b3ch = dt("b3ch", [128, NJ], F32, kind="ExternalInput").ap()
    w1_th = dt("w1_th", [D, 4096], F32, kind="ExternalInput").ap()
    w2_th = dt("w2_th", [4096, D], F32, kind="ExternalInput").ap()
    cwD_h = dt("cwD_h", [128, 96], F32, kind="ExternalInput").ap()
    cbD_h = dt("cbD_h", [128, 32], F32, kind="ExternalInput").ap()
    idf_h = dt("idf_h", [128, 128], F32, kind="ExternalInput").ap()

    out_t = dt("out_t", [D, M], F32, kind="ExternalOutput").ap()

    att_in = dt("att_in", [1, 2064], F32).ap()
    att_out = dt("att_out", [1, 2064], F32).ap()
    gx_in = dt("gx_in", [1, 1], F32).ap()
    gx_out = dt("gx_out", [1, 1], F32).ap()
    gh_in = dt("gh_in", [1, 1], F32).ap()
    gh_out = dt("gh_out", [1, 1], F32).ap()
    h_dram = dt("h_dram", [22, 128, M], F32).ap()
    hq_dram = dt("hq_dram", [22, 128, M], BF16).ap()
    wdq_dram = dt("wdq_dram", [22, 128, 1024], BF16).ap()
    z_dram = dt("z_dram", [NJ, 128, M], F32).ap()
    xc_dram = dt("xc_dram", [NJ, 128, M], F32).ap()
    y_dram = dt("y_dram", [NJ, 128, M], F32).ap()
    cdt_dram = dt("cdt_dram", [NJ, 128, M], F32).ap()
    bc_dram = dt("bc_dram", [32, M], F32).ap()
    st_in = dt("st_in", [128, 128], F32).ap()
    st_out = dt("st_out", [128, 128], F32).ap()
    hx_in = dt("hx_in", [128, 16], F32).ap()
    hx_out = dt("hx_out", [128, 16], F32).ap()
    hD_dram = dt("hD_dram", [32, 128, M], F32).ap()
    h2q_dram = dt("h2q_dram", [32, 128, M], BF16).ap()
    w1q_dram = dt("w1q_dram", [8, 128, 4096], BF16).ap()
    w2q_dram = dt("w2q_dram", [32, 128, 1024], BF16).ap()
    hxD_in = dt("hxD_in", [128, 64], F32).ap()
    hxD_out = dt("hxD_out", [128, 64], F32).ap()

    arena_h = nc.alloc_sbuf_tensor("arena", [128, 52544], F32)
    A0 = nc.lookup_mloc(arena_h).addr
    AT = nc.alloc_sbuf_tensor_at
    xs = AT("xs", [128, NJ * M], F32, offset=A0 + 0).ap()
    aux = AT("aux", [128, NJ * M], F32, offset=A0 + 65536).ap()
    wb = AT("wb", [128, NJ * D], F32, offset=A0 + 131072).ap()
    ch1 = AT("ch1", [128, M], F32, offset=A0 + 163840).ap()
    ch2 = AT("ch2", [128, M], F32, offset=A0 + 172032).ap()
    hbuf = AT("hbuf", [128, M], F32, offset=A0 + 180224).ap()
    brep = AT("brep", [128, M], F32, offset=A0 + 188416).ap()
    crep = AT("crep", [128, M], F32, offset=A0 + 196608).ap()
    pack1 = AT("pack1", [128, 1344], F32, offset=A0 + 204800).ap()
    xqb = AT("xqb", [128, NJ * M], BF16, offset=A0 + 65536).ap()
    bwg = AT("bwg", [128, 4096], F32, offset=A0 + 98304).ap()
    bqw = AT("bqw", [128, 8192], BF16, offset=A0 + 114688).ap()
    h2c = AT("h2c", [128, 8192], F32, offset=A0 + 65536).ap()
    q2c = AT("q2c", [128, 8192], BF16, offset=A0 + 98304).ap()
    outc = AT("outc", [128, 2048], F32, offset=A0 + 114688).ap()
    invrep = AT("invrep", [128, 4096], F32, offset=A0 + 180224).ap()
    ich2 = AT("ich2", [128, M], I32, offset=A0 + 172032).ap()
    icr = AT("icr", [128, M], I32, offset=A0 + 196608).ap()
    hqt = AT("hqt", [128, 2 * M], BF16, offset=A0 + 131072).ap()
    cwt = AT("cwt", [128, 4096], F32, offset=A0 + 114688).ap()
    ich1 = AT("ich1", [128, M], I32, offset=A0 + 163840).ap()
    dqw = AT("dqw", [128, 4096], BF16, offset=A0 + 180224).ap()
    dwe = AT("dwe", [128, 2048], BF16, offset=A0 + 188416).ap()
    dqx = AT("dqx", [128, 2048], BF16, offset=A0 + 172032).ap()
    mD = AT("mD", [128, M], F32, offset=A0 + 65536).ap()

    strow = pack1[0:1, 0:1024]
    rwA = crep[0:1, :]
    rwB = brep[0:1, :]
    mrow = pack1[0:1, 1024:1032]
    lrow = pack1[0:1, 1032:1040]
    mA_r = pack1[0:1, 1040:1048]
    lA_r = pack1[0:1, 1048:1056]
    mB_r = pack1[0:1, 1056:1064]
    lB_r = pack1[0:1, 1064:1072]
    mx_r = pack1[0:1, 1072:1080]
    wA_r = pack1[0:1, 1080:1088]
    wB_r = pack1[0:1, 1088:1096]
    den_r = pack1[0:1, 1096:1104]
    cA_r = pack1[0:1, 1104:1112]
    cB_r = pack1[0:1, 1112:1120]
    ngm = pack1[0:1, 1120:1136]
    tm8 = pack1[0:1, 1136:1144]

    A = nc.alloc_sbuf_tensor
    s8 = A("s8", [8, 16], F32).ap()
    sml = A("sml", [128, 64], F32).ap()
    onr = A("onr", [1, 128], F32).ap()
    onsq = A("onsq", [128, 128], F32).ap()
    attr_s = A("attr_s", [128, 1], F32).ap()
    g0s = A("g0s", [128, NJ], F32).ap()
    b0s = A("b0s", [128, NJ], F32).ap()
    mke = A("mke", [128, 1], F32).ap()
    mko = A("mko", [128, 1], F32).ap()
    cst = A("cst", [128, 6], F32).ap()
    clx = A("clx", [1, 2], F32).ap()
    idb = A("idb", [128, 128], BF16).ap()
    qsc = A("qsc", [128, 16], F32).ap()
    colds = A("colds", [128, NJ], F32).ap()

    P = nc.alloc_psum_tensor
    pbh = [P(f"pb{i}", [128, 512], F32) for i in range(8)]
    pb = [h.ap() for h in pbh]
    pbbf = [h.bitcast(BF16).ap() for h in pbh]

    ACC = sml[:, 0:8]
    ACCA = sml[:, 8:16]
    ACCB = sml[:, 16:24]
    CTX = sml[:, 24:32]
    FH = sml[:, 32:40]
    TMPC = sml[:, 40:48]
    CJ = sml[:, 48:56]
    TC1 = sml[:, 63:64]

    blk = nc.Block()
    block = blk.__enter__()

    def xsl(j, c=None):
        if c is None:
            return xs[:, j * M: (j + 1) * M]
        return xs[:, j * M + c * CSZ: j * M + c * CSZ + 512]

    def auxl(j, c=None):
        if c is None:
            return aux[:, j * M: (j + 1) * M]
        return aux[:, j * M + c * CSZ: j * M + c * CSZ + 512]

    def emit_ln(tag, fused_gamma=False, pre=(), affine=None):
        del tag
        """In-place LayerNorm over d (8 partition-tiles) of the stream in xs.
        fused_gamma: the L0_attn fusion xs <- (xs - fh) + LN(xs)*g + beta,
        where xs holds y = x0 + fh and CJ holds (beta - fh) columns.
        Statistics: sum/sumsq via ones128 matmuls -> PSUM row 0 -> staged to
        an SBUF row -> replicated to (128,512) tiles by a ones-row matmul ->
        all scalar math done on replicated tiles. Uses ch1, ch2, hbuf,
        strow, pb0..7."""
        sq, st, ro, rp, ms, ab, nm = ("Lq", "Lt", "Lr", "Lp", "Lm", "La",
                                      "Ln")
        B = {x: p.v(x) for x in (sq, st, ro, rp, ms, ab, nm)}

        def s_sq(scalar):
            if DEBUG_DUMP == "bxy" and tag == "lB2":
                scalar.wait_ge(p.sem("qout"), 16)
            for nmx, val in pre:
                scalar.wait_ge(p.sem(nmx), val)
            for j in range(NJ):
                if j >= 1:
                    scalar.wait_ge(p.sem(st), B[st] + j)
                scalar.activation(out=ch1[:, :], in_=xsl(j), func=AF.Square)
                p.inc(scalar.drain(), sq)
        block.scalar(s_sq)

        def t_st(tensor):
            for j in range(NJ):
                tensor.wait_ge(p.sem(sq), B[sq] + j + 1)
                for c in range(NCH):
                    tensor.matmul(pb[c][:, :], onsq[:], xsl(j, c),
                                  start=(j == 0), stop=(j == NJ - 1))
                for c in range(NCH):
                    mmi = tensor.matmul(pb[4 + c][:, :], onsq[:],
                                        ch1[:, c * CSZ: c * CSZ + 512],
                                        start=(j == 0), stop=(j == NJ - 1))
                p.inc(mmi, st)
        block.tensor(t_st)

        def s_rows(scalar):
            scalar.wait_ge(p.sem(st), B[st] + NJ)
            for c in range(NCH):
                if c >= 1:
                    scalar.wait_ge(p.sem(rp), B[rp] + c)
                scalar.copy(out=strow[0:1, 0:512], in_=pb[c][0:1, :])
                scalar.copy(out=strow[0:1, 512:1024], in_=pb[4 + c][0:1, :])
                if DEBUG_DUMP == "bst" and tag == "lB2":
                    scalar.copy(out=aux[0:1, c * 1024:(c + 1) * 1024],
                                in_=strow[0:1, 0:1024])
                p.inc(scalar.drain(), ro)
        block.scalar(s_rows)

        def t_rep(tensor):
            for c in range(NCH):
                tensor.wait_ge(p.sem(ro), B[ro] + c + 1)
                tensor.matmul(pb[c][:, :], onr[:], strow[0:1, 0:512],
                              start=True, stop=True)
                p.inc(tensor.matmul(pb[4 + c][:, :], onr[:],
                                    strow[0:1, 512:1024],
                                    start=True, stop=True), rp)
        block.tensor(t_rep)

        def v_ms(vector):
            for c in range(NCH):
                cs = slice(c * CSZ, c * CSZ + 512)
                vector.wait_ge(p.sem(rp), B[rp] + c + 1)
                vector.tensor_scalar(ch2[:, cs], pb[c][:, :], 1.0 / D, None,
                                     OP.mult)
                vector.tensor_scalar(hbuf[:, cs], pb[4 + c][:, :], 1.0 / D,
                                     None, OP.mult)
                vector.drain()
                vector.tensor_tensor(pb[c][:, :], ch2[:, cs], ch2[:, cs],
                                     OP.mult)
                vector.drain()
                vector.tensor_tensor(hbuf[:, cs], hbuf[:, cs],
                                     pb[c][:, :], OP.subtract)
                p.inc(vector.drain(), ms)
        block.vector(v_ms)

        def s_a(scalar):
            for c in range(NCH):
                cs = slice(c * CSZ, c * CSZ + 512)
                scalar.wait_ge(p.sem(ms), B[ms] + c + 1)
                scalar.activation(out=hbuf[:, cs], in_=hbuf[:, cs],
                                  func=AF.Ln, bias=cst[:, 0:1])
                scalar.drain()
                scalar.activation(out=hbuf[:, cs], in_=hbuf[:, cs],
                                  func=AF.Exp, scale=cst[:, 1:2])
                p.inc(scalar.drain(), ab)
        block.scalar(s_a)

        def v_b(vector):
            for c in range(NCH):
                cs = slice(c * CSZ, c * CSZ + 512)
                vector.wait_ge(p.sem(ab), B[ab] + c + 1)
                vector.tensor_tensor(ch2[:, cs], ch2[:, cs], hbuf[:, cs],
                                     OP.mult)
                vector.drain()
                vector.tensor_scalar(ch2[:, cs], ch2[:, cs], -1.0, None,
                                     OP.mult)
                p.inc(vector.drain(), ab)
        block.vector(v_b)

        def v_norm(vector):
            vector.wait_ge(p.sem(ab), B[ab] + 2 * NCH)
            for j in range(NJ):
                for c in range(NCH):
                    cs = slice(c * CSZ, c * CSZ + 512)
                    if affine is not None:
                        ga, be = affine
                        vector.scalar_tensor_tensor(
                            ch1[:, cs], xsl(j, c), ga[:, j:j + 1],
                            hbuf[:, cs], OP.mult, OP.mult)
                        vector.scalar_tensor_tensor(
                            ch1[:, cs], ch2[:, cs], ga[:, j:j + 1],
                            ch1[:, cs], OP.mult, OP.add)
                        p.inc(vector.scalar_tensor_tensor(
                            auxl(j, c), ch1[:, cs], be[:, j:j + 1],
                            ch1[:, cs], OP.add, OP.bypass), nm)
                    elif not fused_gamma:
                        vector.tensor_tensor(ch1[:, cs], xsl(j, c),
                                             hbuf[:, cs], OP.mult)
                        p.inc(vector.tensor_tensor(xsl(j, c), ch1[:, cs],
                                                   ch2[:, cs], OP.add), nm)
                    else:
                        vector.scalar_tensor_tensor(
                            ch1[:, cs], xsl(j, c), g0s[:, j:j + 1],
                            hbuf[:, cs], OP.mult, OP.mult)
                        vector.scalar_tensor_tensor(
                            ch1[:, cs], ch2[:, cs], g0s[:, j:j + 1],
                            ch1[:, cs], OP.mult, OP.add)
                        p.inc(vector.scalar_tensor_tensor(
                            xsl(j, c), xsl(j, c), CJ[:, j:j + 1],
                            ch1[:, cs], OP.add, OP.add), nm)
        block.vector(v_norm)
        return (nm, p.v(nm))

    def a_sync(sync):
        p.inc(sync.dma_start(out=xs.rearrange("p (j t) -> p j t", j=NJ),
                             in_=x_in.rearrange("(j p) t -> p j t", p=128)), "qs", 16)
        p.inc(sync.dma_start(out=wb.rearrange("p (j e) -> p j e", j=NJ),
                             in_=wr_t.rearrange("(j p) e -> p j e", p=128)), "qs", 16)
        p.inc(sync.dma_start(out=onr[:], in_=ones_r), "qs", 16)
        p.inc(sync.dma_start(out=onsq[:], in_=ones_sq), "qs", 16)
        p.inc(sync.dma_start(out=attr_s[:], in_=attr_c), "qs", 16)
        p.inc(sync.dma_start(out=g0s[:], in_=g0c), "qs", 16)
        p.inc(sync.dma_start(out=b0s[:], in_=b0c), "qs", 16)
        p.inc(sync.dma_start(out=mke[:], in_=mask_e), "qs", 16)
        p.inc(sync.dma_start(out=mko[:], in_=mask_o), "qs", 16)
        p.inc(sync.dma_start(out=cst[:], in_=constc), "qs", 16)
        p.inc(sync.dma_start(out=clx[:], in_=clipx), "qs", 16)
        p.inc(sync.dma_start(out=idb[:], in_=identb), "qs", 16)
    block.sync(a_sync)
    LOADS_A = p.v("qs")

    mm_at = {}

    def a_tensor(tensor):
        tensor.wait_ge(p.sem("qs"), LOADS_A)
        k = 0
        for e in range(NJ):
            for c in range(NCH):
                if k >= 2:
                    tensor.wait_ge(p.sem("sa"), k - 1)
                    tensor.wait_ge(p.sem("sv"), k - 1)
                for j in range(NJ):
                    mm = tensor.matmul(pb[k % 2][:, :],
                                       wb[:, j * D + e * 128: j * D + e * 128 + 128],
                                       xsl(j, c), start=(j == 0), stop=(j == NJ - 1))
                p.inc(mm, "mm")
                mm_at[(e, c)] = p.v("mm")
                k += 1
            for c in range(NCH):
                i = e * NCH + c
                tensor.wait_ge(p.sem("sa"), i + 1)
                if i >= 1:
                    tensor.wait_ge(p.sem("sre"), i)
                p.inc(tensor.matmul(pb[2][0:1, :], attr_s[:],
                                    ch1[:, c * CSZ: c * CSZ + 512],
                                    start=True, stop=True), "msr")
            if e >= 1:
                eh = e - 1
                row = rwA if eh % 2 == 0 else rwB
                tensor.wait_ge(p.sem("swx"), eh + 1)
                tensor.wait_ge(p.sem("vl"), eh + 1)
                for c in range(NCH):
                    kk = eh * NCH + c
                    if kk >= 2:
                        tensor.wait_ge(p.sem("sva"), kk - 1)
                    p.inc(tensor.matmul(pb[3 + (kk % 2)][:, :], onr[:],
                                        row[0:1, c * CSZ: c * CSZ + 512],
                                        start=True, stop=True), "mrep")
        eh = NJ - 1
        row = rwA if eh % 2 == 0 else rwB
        tensor.wait_ge(p.sem("swx"), eh + 1)
        tensor.wait_ge(p.sem("vl"), eh + 1)
        for c in range(NCH):
            kk = eh * NCH + c
            tensor.wait_ge(p.sem("sva"), kk - 1)
            p.inc(tensor.matmul(pb[3 + (kk % 2)][:, :], onr[:],
                                row[0:1, c * CSZ: c * CSZ + 512],
                                start=True, stop=True), "mrep")
    block.tensor(a_tensor)
    MM_A = p.v("mm")

    def a_scalar(scalar):
        for e in range(NJ):
            row = rwA if e % 2 == 0 else rwB
            if e >= 2:
                scalar.wait_ge(p.sem("mrep"), 4 * (e - 1))
                scalar.wait_ge(p.sem("vl"), e - 1)
            for c in range(NCH):
                i = e * NCH + c
                scalar.wait_ge(p.sem("msr"), i + 1)
                p.inc(scalar.copy(out=row[0:1, c * CSZ: c * CSZ + 512],
                                  in_=pb[2][0:1, :]), "sre")
            scalar.wait_ge(p.sem("sre"), (e + 1) * NCH)
            scalar.activation(out=row, in_=row, func=AF.Exp, bias=cst[0:1, 2:3])
            p.inc(scalar.drain(), "swx")
    block.scalar(a_scalar)

    PARTS = sml[:, 56:60]

    def a_vector(vector):
        def acc_head(vector, eh):
            for c in range(NCH):
                kk = eh * NCH + c
                vector.wait_ge(p.sem("mrep"), kk + 1)
                p.inc(vector.scalar_tensor_tensor(
                    ch2[:, 0:512], auxl(eh, c), 1.0, pb[3 + (kk % 2)][:, :],
                    OP.bypass, OP.mult, accum_out=PARTS[:, c:c + 1]), "sva")
            vector.wait_ge(p.sem("sva"), (eh + 1) * NCH)
            vector.drain()
            vector.tensor_tensor(ACC[:, eh:eh + 1], PARTS[:, 0:1],
                                 PARTS[:, 1:2], OP.add)
            vector.tensor_tensor(TC1, PARTS[:, 2:3], PARTS[:, 3:4], OP.add)
            vector.drain()
            p.inc(vector.tensor_tensor(ACC[:, eh:eh + 1], ACC[:, eh:eh + 1],
                                       TC1, OP.add), "sva2")

        k = 0
        for e in range(NJ):
            row = rwA if e % 2 == 0 else rwB
            for c in range(NCH):
                vector.wait_ge(p.sem("mm"), mm_at[(e, c)])
                p.inc(vector.tensor_copy(auxl(e, c), pb[k % 2][:, :]), "sv")
                if e >= 1:
                    vector.wait_ge(p.sem("msr"), (e - 1) * NCH + c + 1)
                vector.drain()
                p.inc(vector.scalar_tensor_tensor(
                    ch1[:, c * CSZ: c * CSZ + 512], auxl(e, c), 0.01,
                    auxl(e, c), OP.mult, OP.max), "sa")
                k += 1
            if e >= 1:
                prow = rwB if e % 2 == 0 else rwA
                vector.wait_ge(p.sem("swx"), e)
                p.inc(vector.reduce_sum(lrow[0:1, e - 1:e], prow, axis=AX),
                      "vl")
                acc_head(vector, e - 1)
        vector.wait_ge(p.sem("swx"), NJ)
        p.inc(vector.reduce_sum(lrow[0:1, NJ - 1:NJ],
                                rwB if (NJ - 1) % 2 else rwA, axis=AX), "vl")
        acc_head(vector, NJ - 1)
        vector.wait_ge(p.sem("vl"), NJ)
        vector.wait_ge(p.sem("sva2"), NJ)
        vector.drain()
        vector.scalar_tensor_tensor(ACCA[:, :], ACC[:, :], mke[:, 0:1],
                                    ACC[:, :], OP.mult, OP.bypass)
        vector.scalar_tensor_tensor(ACCB[:, :], ACC[:, :], mko[:, 0:1],
                                    ACC[:, :], OP.mult, OP.bypass)
        vector.scalar_tensor_tensor(lA_r, lrow, mke[0:1, 0:1], lrow,
                                    OP.mult, OP.bypass)
        vector.scalar_tensor_tensor(lB_r, lrow, mko[0:1, 0:1], lrow,
                                     OP.mult, OP.bypass)
        p.inc(vector.drain(), "sv")
    block.vector(a_vector)
    SV_PACK = p.v("sv")

    def a4_gpsimd(g):
        g.wait_ge(p.sem("sv"), SV_PACK)
        p.inc(g.dma_start(out=att_in[0:1, 0:1024].rearrange("1 (q j) -> q j", q=128),
                          in_=ACCA), "qg", 16)
        p.inc(g.dma_start(out=att_in[0:1, 1024:1032], in_=lA_r), "qg", 16)
        p.inc(g.dma_start(out=att_in[0:1, 1032:2056].rearrange("1 (q j) -> q j", q=128),
                          in_=ACCB), "qg", 16)
        p.inc(g.dma_start(out=att_in[0:1, 2056:2064], in_=lB_r), "qg", 16)
        g.wait_ge(p.sem("qg"), p.v("qg"))
        p.inc(g.collective_compute(
            "AllReduce", OP.add,
            replica_groups=[[0, 1], [2, 3], [4, 5], [6, 7]],
            ins=[att_in], outs=[att_out]), "cc")
        g.wait_ge(p.sem("cc"), p.v("cc"))
        p.inc(g.dma_start(out=ACCA, in_=att_out[0:1, 0:1024].rearrange("1 (q j) -> q j", q=128)), "qg", 16)
        p.inc(g.dma_start(out=lA_r, in_=att_out[0:1, 1024:1032]), "qg", 16)
        p.inc(g.dma_start(out=ACCB, in_=att_out[0:1, 1032:2056].rearrange("1 (q j) -> q j", q=128)), "qg", 16)
        p.inc(g.dma_start(out=lB_r, in_=att_out[0:1, 2056:2064]), "qg", 16)
    block.gpsimd(a4_gpsimd)
    QG_ATT = p.v("qg")

    def a6_vector(vector):
        vector.wait_ge(p.sem("qg"), QG_ATT)
        vector.tensor_tensor(den_r, lA_r, lB_r, OP.add)
        vector.drain()
        vector.reciprocal(den_r, den_r)
        vector.drain()
        vector.tensor_copy(cA_r, den_r)
        vector.tensor_copy(cB_r, den_r)
        p.inc(vector.drain(), "sv")
    block.vector(a6_vector)
    SV_C = p.v("sv")

    def a7_tensor(tensor):
        tensor.wait_ge(p.sem("sv"), SV_C)
        tensor.matmul(pb[5][:, 0:8], onr[:], cA_r, start=True, stop=True)
        p.inc(tensor.matmul(pb[6][:, 0:8], onr[:], cB_r,
                            start=True, stop=True), "mrep")
    block.tensor(a7_tensor)
    MM_CREP = p.v("mrep")

    def a7_vector(vector):
        vector.wait_ge(p.sem("mrep"), MM_CREP)
        vector.tensor_tensor(CTX[:, :], ACCA[:, :], pb[5][:, 0:8], OP.mult)
        vector.tensor_tensor(TMPC[:, :], ACCB[:, :], pb[6][:, 0:8], OP.mult)
        vector.drain()
        vector.tensor_tensor(CTX[:, :], CTX[:, :], TMPC[:, :], OP.add)
        p.inc(vector.drain(), "sv")
    block.vector(a7_vector)
    SV_CTX = p.v("sv")

    def a8_sync(sync):
        sync.wait_ge(p.sem("mm"), MM_A)
        p.inc(sync.dma_start(out=wb.rearrange("p (j e) -> p j e", j=NJ),
                             in_=wf_t.rearrange("(j p) e -> p j e", p=128)),
              "qs", 16)
    block.sync(a8_sync)
    QS_WF = p.v("qs")

    def a8_tensor(tensor):
        tensor.wait_ge(p.sem("sv"), SV_CTX)
        tensor.wait_ge(p.sem("qs"), QS_WF)
        for e2 in range(NJ):
            if e2 >= 1:
                tensor.wait_ge(p.sem("sfh"), e2)
            for j in range(NJ):
                mmi = tensor.matmul(pb[7][:, 0:1],
                                    wb[:, j * D + e2 * 128: j * D + e2 * 128 + 128],
                                    CTX[:, j:j + 1],
                                    start=(j == 0), stop=(j == NJ - 1))
            p.inc(mmi, "mfh")
    block.tensor(a8_tensor)

    def a8_scalar(scalar):
        for e2 in range(NJ):
            scalar.wait_ge(p.sem("mfh"), e2 + 1)
            p.inc(scalar.copy(out=FH[:, e2:e2 + 1], in_=pb[7][:, 0:1]), "sfh")
    block.scalar(a8_scalar)

    def a9_vector(vector):
        vector.wait_ge(p.sem("sfh"), NJ)
        vector.tensor_tensor(CJ[:, :], b0s[:, :], FH[:, :], OP.subtract)
        vector.drain()
        for j in range(NJ):
            p.inc(vector.scalar_tensor_tensor(xsl(j), xsl(j), FH[:, j:j + 1],
                                              xsl(j), OP.add, OP.bypass),
                  "ay")
        p.inc(vector.drain(), "ay")
    block.vector(a9_vector)

    nmA = emit_ln("lA", fused_gamma=True, pre=[("ay", NJ + 1)])
    nmB = emit_ln("lB", fused_gamma=False, pre=[nmA])

    NI = 22

    def b0_vector(vector):
        vector.wait_ge(p.sem(nmB[0]), nmB[1])
        vector.drain()
        for j in range(NJ):
            vector.reduce_max(sml[:, 56:57], xsl(j), axis=AX,
                              apply_absolute_value=True)
            vector.drain()
            if j == 0:
                vector.tensor_copy(sml[:, 58:59], sml[:, 56:57])
            else:
                vector.tensor_tensor(sml[:, 58:59], sml[:, 58:59],
                                     sml[:, 56:57], OP.max)
            vector.drain()
        p.inc(vector.drain(), "bx")
    block.vector(b0_vector)

    def b0_gpsimd(g):
        g.wait_ge(p.sem("bx"), 1)
        p.inc(g.dma_start(out=pack1[0:1, 1152:1280], in_=sml[:, 58:59]),
              "qg", 16)
    block.gpsimd(b0_gpsimd)
    QG_BX = p.v("qg")

    def b0v2(vector):
        vector.wait_ge(p.sem("qg"), QG_BX)
        vector.reduce_max(pack1[0:1, 1280:1281], pack1[0:1, 1152:1280], axis=AX)
        p.inc(vector.drain(), "bx")
    block.vector(b0v2)

    def b0_g2(g):
        g.wait_ge(p.sem("bx"), 2)
        p.inc(g.dma_start(out=gx_in[0:1, 0:1], in_=pack1[0:1, 1280:1281]),
              "qg", 16)
        g.wait_ge(p.sem("qg"), p.v("qg"))
        p.inc(g.collective_compute("AllReduce", OP.max,
                                   replica_groups=[list(range(8))],
                                   ins=[gx_in], outs=[gx_out]), "cc")
        g.wait_ge(p.sem("cc"), p.v("cc"))
        p.inc(g.dma_start(out=pack1[0:1, 1281:1282], in_=gx_out[0:1, 0:1]),
              "qg", 16)
    block.gpsimd(b0_g2)
    QG_GX = p.v("qg")

    def b1_scalar(scalar):
        scalar.wait_ge(p.sem("qg"), QG_GX)
        scalar.activation(out=pack1[0:1, 1282:1283], in_=clx[0:1, 0:1],
                          func=AF.Sigmoid)
        p.inc(scalar.drain(), "bs")
    block.scalar(b1_scalar)
    BS_SIG = p.v("bs")

    def b1_vector(vector):
        vector.wait_ge(p.sem("bs"), BS_SIG)
        vector.tensor_tensor(pack1[0:1, 1283:1284], pack1[0:1, 1282:1283],
                             pack1[0:1, 1281:1282], OP.mult)
        vector.drain()
        vector.reciprocal(pack1[0:1, 1284:1285], pack1[0:1, 1283:1284])
        vector.drain()
        vector.tensor_scalar(pack1[0:1, 1285:1286], pack1[0:1, 1284:1285],
                             127.0, None, OP.mult)
        vector.tensor_scalar(pack1[0:1, 1286:1287], pack1[0:1, 1283:1284],
                             1.0 / 16129.0, None, OP.mult)
        p.inc(vector.drain(), "bv")
    block.vector(b1_vector)
    BV1 = p.v("bv")

    def b1_tensor(tensor):
        tensor.wait_ge(p.sem("bv"), BV1)
        tensor.matmul(pb[0][:, 0:1], onr[:], pack1[0:1, 1285:1286],
                      start=True, stop=True)
        p.inc(tensor.matmul(pb[1][:, 0:1], onr[:], pack1[0:1, 1286:1287],
                            start=True, stop=True), "bm")
    block.tensor(b1_tensor)
    BM1 = p.v("bm")

    def b1_s2(scalar):
        scalar.wait_ge(p.sem("bm"), BM1)
        scalar.copy(out=qsc[:, 0:1], in_=pb[0][:, 0:1])
        scalar.copy(out=qsc[:, 1:2], in_=pb[1][:, 0:1])
        p.inc(scalar.drain(), "bs")
    block.scalar(b1_s2)
    BS_Q = p.v("bs")

    def b2_vector(vector):
        vector.wait_ge(p.sem("bs"), BS_Q)
        for j in range(NJ):
            vector.scalar_tensor_tensor(ich2[:, :], xsl(j), qsc[:, 0:1],
                                        xsl(j), OP.mult, OP.bypass)
            vector.drain()
            vector.tensor_scalar(xqb[:, j * M:(j + 1) * M], ich2[:, :],
                                 -128.0, 127.0, OP.max, OP.min)
            p.inc(vector.drain(), "bq")
    block.vector(b2_vector)

    def bw_sync(sync):
        sync.wait_ge(p.sem("bs"), BS_Q)
        for i in range(NI):
            sl = (i % 2) * 2048
            if i >= 2:
                sync.wait_ge(p.sem("bwv"), (i - 1) * 2)
            p.inc(sync.dma_start(out=bwg[:, sl:sl + 1024],
                                 in_=wg_n[i * 128:(i + 1) * 128, :]), "qw", 16)
            p.inc(sync.dma_start(out=bwg[:, sl + 1024:sl + 2048],
                                 in_=wu_n[i * 128:(i + 1) * 128, :]), "qw", 16)
    block.sync(bw_sync)
    QW_A0 = p.v("qw") - 32 * NI

    def bw_vector(vector):
        def h_tile(vector, i):
            for c in range(NCH):
                k = i * NCH + c
                vector.wait_ge(p.sem("bev"), k + 1)
                vector.tensor_tensor(hbuf[:, c * CSZ: c * CSZ + 512],
                                     ch1[:, c * CSZ: c * CSZ + 512],
                                     ch2[:, c * CSZ: c * CSZ + 512], OP.mult)
                vector.drain()
                vector.reduce_max(sml[:, 59:60],
                                  hbuf[:, c * CSZ: c * CSZ + 512], axis=AX,
                                  apply_absolute_value=True)
                vector.drain()
                if k == 0:
                    vector.tensor_copy(sml[:, 60:61], sml[:, 59:60])
                else:
                    vector.tensor_tensor(sml[:, 60:61], sml[:, 60:61],
                                         sml[:, 59:60], OP.max)
                p.inc(vector.drain(), "bhv")

        for i in range(NI):
            sl = (i % 2) * 2048
            vector.wait_ge(p.sem("qw"), QW_A0 + 32 * (i + 1))
            if i >= 1:
                vector.wait_ge(p.sem("bwt"), i * 16)
            if i >= 2:
                vector.wait_ge(p.sem("qh"), 16 * (i - 1))
            for gu in range(2):
                w_nat = bwg[:, sl + gu * 1024: sl + gu * 1024 + 1024]
                vector.reduce_max(qsc[:, 6 + gu:7 + gu], w_nat, axis=AX,
                                  apply_absolute_value=True)
                vector.drain()
                vector.scalar_tensor_tensor(qsc[:, 2 + gu:3 + gu],
                                            qsc[:, 6 + gu:7 + gu],
                                            qsc[:, 1:2], qsc[:, 1:2],
                                            OP.mult, OP.bypass)
                vector.reciprocal(qsc[:, 8 + gu:9 + gu], qsc[:, 6 + gu:7 + gu])
                vector.drain()
                vector.tensor_scalar(qsc[:, 8 + gu:9 + gu],
                                     qsc[:, 8 + gu:9 + gu], 127.0, None,
                                     OP.mult)
                vector.drain()
                vector.scalar_tensor_tensor(icr[:, 0:1024], w_nat,
                                            qsc[:, 8 + gu:9 + gu], w_nat,
                                            OP.mult, OP.bypass)
                vector.drain()
                vector.tensor_scalar(hqt[:, gu * 1024:(gu + 1) * 1024],
                                     icr[:, 0:1024], -128.0, 127.0,
                                     OP.max, OP.min)
                p.inc(vector.drain(), "bwv")
            if i >= 2:
                vector.wait_ge(p.sem("bev"), 4 * (i - 1))
            vector.tensor_copy(qsc[:, 10 + 2 * (i % 2):11 + 2 * (i % 2)],
                               qsc[:, 2:3])
            vector.tensor_copy(qsc[:, 11 + 2 * (i % 2):12 + 2 * (i % 2)],
                               qsc[:, 3:4])
            p.inc(vector.drain(), "bcol")
            if i >= 1:
                h_tile(vector, i - 1)
        h_tile(vector, NI - 1)
    block.vector(bw_vector)

    def bw_tensor(tensor):
        for i in range(NI):
            tensor.wait_ge(p.sem("bwv"), i * 2 + 2)
            for t in range(16):
                gu, j = divmod(t, 8)
                tt = i * 16 + t
                if tt >= 4:
                    tensor.wait_ge(p.sem("bts"), tt - 3)
                p.inc(tensor.transpose(pbbf[4 + (t % 4)][:, 0:128],
                                       hqt[:, gu * 1024 + j * 128: gu * 1024 + j * 128 + 128],
                                       idb[:, :]), "bwt")
            tensor.wait_ge(p.sem("bts"), (i + 1) * 16)
            if i == 0:
                tensor.wait_ge(p.sem("bq"), NJ)
            for c in range(NCH):
                if i * NCH + c >= 2:
                    tensor.wait_ge(p.sem("bhv"), i * NCH + c - 1)
                for j in range(NJ):
                    tensor.matmul(pb[0 + ((i * NCH + c) % 2)][:, :],
                                  bqw[:, j * 128:(j + 1) * 128],
                                  xqb[:, j * M + c * CSZ: j * M + c * CSZ + 512],
                                  start=(j == 0), stop=(j == NJ - 1))
                for j in range(NJ):
                    mm = tensor.matmul(pb[2 + ((i * NCH + c) % 2)][:, :],
                                       bqw[:, 1024 + j * 128: 1024 + j * 128 + 128],
                                       xqb[:, j * M + c * CSZ: j * M + c * CSZ + 512],
                                       start=(j == 0), stop=(j == NJ - 1))
                p.inc(mm, "bmm")
    block.tensor(bw_tensor)

    def bts_scalar(scalar):
        for i in range(NI):
            for t in range(16):
                gu, j = divmod(t, 8)
                tt = i * 16 + t
                scalar.wait_ge(p.sem("bwt"), tt + 1)
                scalar.copy(out=bqw[:, gu * 1024 + j * 128: gu * 1024 + j * 128 + 128],
                            in_=pbbf[4 + (t % 4)][:, 0:128])
                p.inc(scalar.drain(), "bts")
            for c in range(NCH):
                k = i * NCH + c
                scalar.wait_ge(p.sem("bmm"), k + 1)
                scalar.wait_ge(p.sem("bcol"), i + 1)
                scalar.activation(out=ch1[:, c * CSZ: c * CSZ + 512],
                                  in_=pb[0 + (k % 2)][:, :], func=AF.Silu,
                                  scale=qsc[:, 10 + 2 * (i % 2):11 + 2 * (i % 2)])
                scalar.activation(out=ch2[:, c * CSZ: c * CSZ + 512],
                                  in_=pb[2 + (k % 2)][:, :], func=AF.Copy,
                                  scale=qsc[:, 11 + 2 * (i % 2):12 + 2 * (i % 2)])
                p.inc(scalar.drain(), "bev")
    block.scalar(bts_scalar)

    def bh_sync(g):
        for i in range(NI):
            g.wait_ge(p.sem("bhv"), i * NCH + NCH)
            p.inc(g.dma_start(out=h_dram[i], in_=hbuf[:, :]), "qh", 16)
    block.gpsimd(bh_sync)
    QH_SP = p.v("qh")

    if DEBUG_DUMP == "bxw":
        def dbg_syncw(sync):
            sync.wait_ge(p.sem("bhv"), NI * NCH)
            p.inc(sync.dma_start(out=out_t.rearrange("(j p) t -> p j t", p=128),
                                 in_=xs.rearrange("p (j t) -> p j t", j=NJ)),
                  "qout", 16)
        block.sync(dbg_syncw)

    def b5_gpsimd(g):
        if DEBUG_DUMP == "bxw":
            g.wait_ge(p.sem("qout"), 16)
        g.wait_ge(p.sem("bhv"), NI * NCH)
        p.inc(g.dma_start(out=pack1[0:1, 1152:1280], in_=sml[:, 60:61]),
              "qg", 16)
    block.gpsimd(b5_gpsimd)
    QG_BH = p.v("qg")

    def b5_vector(vector):
        vector.wait_ge(p.sem("qg"), QG_BH)
        vector.reduce_max(pack1[0:1, 1287:1288], pack1[0:1, 1152:1280], axis=AX)
        p.inc(vector.drain(), "bv")
    block.vector(b5_vector)
    BV_H = p.v("bv")

    def b5_g2(g):
        g.wait_ge(p.sem("bv"), BV_H)
        p.inc(g.dma_start(out=gh_in[0:1, 0:1], in_=pack1[0:1, 1287:1288]),
              "qg", 16)
        g.wait_ge(p.sem("qg"), p.v("qg"))
        p.inc(g.collective_compute("AllReduce", OP.max,
                                   replica_groups=[list(range(8))],
                                   ins=[gh_in], outs=[gh_out]), "cc")
        g.wait_ge(p.sem("cc"), p.v("cc"))
        p.inc(g.dma_start(out=pack1[0:1, 1288:1289], in_=gh_out[0:1, 0:1]),
              "qg", 16)
    block.gpsimd(b5_g2)
    QG_GH = p.v("qg")

    def b5_scalar(scalar):
        scalar.wait_ge(p.sem("qg"), QG_GH)
        scalar.activation(out=pack1[0:1, 1289:1290], in_=clx[0:1, 1:2],
                          func=AF.Sigmoid)
        p.inc(scalar.drain(), "bs")
    block.scalar(b5_scalar)
    BS_SIG2 = p.v("bs")

    def b5v2(vector):
        vector.wait_ge(p.sem("bs"), BS_SIG2)
        vector.tensor_tensor(pack1[0:1, 1290:1291], pack1[0:1, 1289:1290],
                             pack1[0:1, 1288:1289], OP.mult)
        vector.drain()
        vector.reciprocal(pack1[0:1, 1291:1292], pack1[0:1, 1290:1291])
        vector.drain()
        vector.tensor_scalar(pack1[0:1, 1292:1293], pack1[0:1, 1291:1292],
                             127.0, None, OP.mult)
        vector.tensor_scalar(pack1[0:1, 1293:1294], pack1[0:1, 1290:1291],
                             1.0 / 16129.0, None, OP.mult)
        p.inc(vector.drain(), "bv")
    block.vector(b5v2)
    BV_H2 = p.v("bv")

    def b5_t2(tensor):
        tensor.wait_ge(p.sem("bv"), BV_H2)
        tensor.matmul(pb[0][:, 0:1], onr[:], pack1[0:1, 1292:1293],
                      start=True, stop=True)
        p.inc(tensor.matmul(pb[1][:, 0:1], onr[:], pack1[0:1, 1293:1294],
                            start=True, stop=True), "bm")
    block.tensor(b5_t2)
    BM2 = p.v("bm")

    def b5_s3(scalar):
        scalar.wait_ge(p.sem("bm"), BM2)
        scalar.copy(out=qsc[:, 4:5], in_=pb[0][:, 0:1])
        scalar.copy(out=qsc[:, 5:6], in_=pb[1][:, 0:1])
        p.inc(scalar.drain(), "bs")
    block.scalar(b5_s3)
    BS_H = p.v("bs")

    def b6_sync(g):
        g.wait_ge(p.sem("bs"), BS_H)
        for i in range(NI):
            if i >= 2:
                g.wait_ge(p.sem("bhq"), i - 1)
            g.wait_ge(p.sem("qh"), QH_SP - 16 * NI + 16 * (i + 1))
            p.inc(g.dma_start(out=hbuf[:, :] if i % 2 == 0 else brep[:, 0:M],
                              in_=h_dram[i]), "qh2", 16)
            if i >= 1:
                g.wait_ge(p.sem("bhq"), i)
                p.inc(g.dma_start(out=hq_dram[i - 1],
                                  in_=hqt[:, 0:M] if (i - 1) % 2 == 0 else hqt[:, M:2 * M]),
                      "qhq", 16)
        g.wait_ge(p.sem("bhq"), NI)
        p.inc(g.dma_start(out=hq_dram[NI - 1],
                          in_=hqt[:, 0:M] if (NI - 1) % 2 == 0 else hqt[:, M:2 * M]),
              "qhq", 16)
    block.gpsimd(b6_sync)

    def b6_vector(vector):
        for i in range(NI):
            buf = hbuf[:, :] if i % 2 == 0 else brep[:, 0:M]
            vector.wait_ge(p.sem("qh2"), 16 * (i + 1))
            if i >= 2:
                vector.wait_ge(p.sem("qhq"), 16 * (i - 1))
            vector.scalar_tensor_tensor(ich2[:, :], buf, qsc[:, 4:5], buf,
                                        OP.mult, OP.bypass)
            vector.drain()
            vector.tensor_scalar(hqt[:, 0:M] if i % 2 == 0 else hqt[:, M:2 * M],
                                 ich2[:, :], -128.0, 127.0, OP.max, OP.min)
            p.inc(vector.drain(), "bhq")
    block.vector(b6_vector)

    QHQ = p.v("qhq")

    if DEBUG_DUMP == "bx6":
        def dbg_sync6(sync):
            sync.wait_ge(p.sem("qhq"), QHQ)
            p.inc(sync.dma_start(out=out_t.rearrange("(j p) t -> p j t", p=128),
                                 in_=xs.rearrange("p (j t) -> p j t", j=NJ)),
                  "qout", 16)
        block.sync(dbg_sync6)

    def b7_sync(sync):
        if DEBUG_DUMP == "bx6":
            sync.wait_ge(p.sem("qout"), 16)
        sync.wait_ge(p.sem("bhq"), NI)
        for j in range(NJ):
            if j >= 1:
                sync.wait_ge(p.sem("bwd_done"), j)
            p.inc(sync.dma_start(out=invrep[:, 0:2816],
                                 in_=wd_n[j * 128:(j + 1) * 128, :]), "qw", 16)
    block.sync(b7_sync)
    QW_B0 = p.v("qw") - 16 * NJ

    def b7_vector(vector):
        vector.wait_ge(p.sem("qhq"), QHQ)
        for j in range(NJ):
            vector.wait_ge(p.sem("qw"), QW_B0 + 16 * (j + 1))
            if j >= 1:
                vector.wait_ge(p.sem("bwd_t"), j * NI)
            vector.reduce_max(qsc[:, 6:7], invrep[:, 0:2816], axis=AX,
                              apply_absolute_value=True)
            vector.drain()
            vector.scalar_tensor_tensor(qsc[:, 12:13], qsc[:, 6:7],
                                        qsc[:, 5:6], qsc[:, 5:6],
                                        OP.mult, OP.bypass)
            vector.reciprocal(qsc[:, 7:8], qsc[:, 6:7])
            vector.drain()
            vector.tensor_scalar(qsc[:, 7:8], qsc[:, 7:8], 127.0, None, OP.mult)
            vector.drain()
            vector.tensor_copy(colds[:, j:j + 1], qsc[:, 12:13])
            vector.drain()
            for half in range(2):
                lo, hi = half * 1408, half * 1408 + 1408
                vector.scalar_tensor_tensor(icr[:, 0:1408],
                                            invrep[:, lo:hi], qsc[:, 7:8],
                                            invrep[:, lo:hi], OP.mult,
                                            OP.bypass)
                vector.drain()
                vector.tensor_scalar(hqt[:, lo:hi], icr[:, 0:1408],
                                     -128.0, 127.0, OP.max, OP.min)
                vector.drain()
            p.inc(vector.drain(), "bwdq")
    block.vector(b7_vector)

    def b7_tensor(tensor):
        for j in range(NJ):
            tensor.wait_ge(p.sem("bwdq"), j + 1)
            for i in range(NI):
                if (j * NI + i) >= 2:
                    tensor.wait_ge(p.sem("bwd_ev"), j * NI + i - 1)
                p.inc(tensor.transpose(pbbf[4 + ((j * NI + i) % 2)][:, 0:128],
                                       hqt[:, i * 128:(i + 1) * 128],
                                       idb[:, :]), "bwd_t")
    block.tensor(b7_tensor)

    def b7_scalar(scalar):
        for j in range(NJ):
            for i in range(NI):
                k = j * NI + i
                scalar.wait_ge(p.sem("bwd_t"), k + 1)
                if k >= 8:
                    scalar.wait_ge(p.sem("qwd"), 16 * (k - 7))
                p.inc(scalar.copy(out=bqw[:, (k % 8) * 128:(k % 8) * 128 + 128],
                                  in_=pbbf[4 + (k % 2)][:, 0:128]), "bwd_ev")
            p.inc(scalar.drain(), "bwd_done")
    block.scalar(b7_scalar)

    def b7_g(g):
        for j in range(NJ):
            for i in range(NI):
                k = j * NI + i
                g.wait_ge(p.sem("bwd_ev"), k + 1)
                p.inc(g.dma_start(out=wdq_dram[i][:, j * 128:(j + 1) * 128],
                                  in_=bqw[:, (k % 8) * 128:(k % 8) * 128 + 128]),
                      "qwd", 16)
    block.gpsimd(b7_g)
    QWD = p.v("qwd")

    if DEBUG_DUMP == "bx7":
        def dbg_sync7(sync):
            sync.wait_ge(p.sem("qwd"), QWD)
            p.inc(sync.dma_start(out=out_t.rearrange("(j p) t -> p j t", p=128),
                                 in_=xs.rearrange("p (j t) -> p j t", j=NJ)),
                  "qout", 16)
        block.sync(dbg_sync7)

    def b8_sync(sync):
        if DEBUG_DUMP == "bx7":
            sync.wait_ge(p.sem("qout"), 16)
        sync.wait_ge(p.sem("qwd"), QWD)
        sync.wait_ge(p.sem("qhq"), QHQ)
        for c in range(NCH):
            for i in range(NI):
                k = c * NI + i
                if k >= 2:
                    sync.wait_ge(p.sem("bdm"), k - 1)
                p.inc(sync.dma_start(
                    out=bqw[:, (k % 2) * 1024:(k % 2) * 1024 + 1024],
                    in_=wdq_dram[i]), "qw", 16)
                p.inc(sync.dma_start(
                    out=bqw[:, 2048 + (k % 2) * 512: 2048 + (k % 2) * 512 + 512],
                    in_=hq_dram[i][:, c * CSZ: c * CSZ + 512]), "qw", 16)
    block.sync(b8_sync)
    QW_C0 = p.v("qw") - 32 * NCH * NI

    def b8_tensor(tensor):
        if DEBUG_DUMP == "bxl":
            tensor.wait_ge(p.sem("qout"), 16)
        for c in range(NCH):
            if c >= 1:
                tensor.wait_ge(p.sem("bde"), c * NJ)
            for i in range(NI):
                k = c * NI + i
                tensor.wait_ge(p.sem("qw"), QW_C0 + 32 * (k + 1))
                for j in range(NJ):
                    mm = tensor.matmul(
                        pb[j][:, :],
                        bqw[:, (k % 2) * 1024 + j * 128:(k % 2) * 1024 + j * 128 + 128],
                        bqw[:, 2048 + (k % 2) * 512: 2048 + (k % 2) * 512 + 512],
                        start=(i == 0), stop=(i == NI - 1))
                p.inc(mm, "bdm")
    block.tensor(b8_tensor)

    def b8_scalar(scalar):
        if DEBUG_DUMP == "bxm":
            scalar.wait_ge(p.sem("qout"), 16)
        for c in range(NCH):
            scalar.wait_ge(p.sem("bdm"), (c + 1) * NI)
            for j in range(NJ):
                k = c * NJ + j
                if k >= 1:
                    scalar.wait_ge(p.sem("bde"), k)
                scalar.activation(out=ch2[:, 0:512], in_=pb[j][:, :],
                                  func=AF.Copy, scale=colds[:, j:j + 1])
                p.inc(scalar.drain(), "bds")
    block.scalar(b8_scalar)

    if DEBUG_DUMP == "bxl":
        def dbg_gl(g):
            g.wait_ge(p.sem("qw"), QW_C0 + 32 * 2)
            p.inc(g.dma_start(out=out_t.rearrange("(j p) t -> p j t", p=128),
                              in_=xs.rearrange("p (j t) -> p j t", j=NJ)),
                  "qout", 16)
        block.gpsimd(dbg_gl)

    if DEBUG_DUMP == "bxm":
        def dbg_gm(g):
            g.wait_ge(p.sem("bdm"), NI)
            p.inc(g.dma_start(out=out_t.rearrange("(j p) t -> p j t", p=128),
                              in_=xs.rearrange("p (j t) -> p j t", j=NJ)),
                  "qout", 16)
        block.gpsimd(dbg_gm)

    if DEBUG_DUMP == "bxy":
        def dbg_gy(g):
            g.wait_ge(p.sem("bde"), NCH * NJ)
            p.inc(g.dma_start(out=out_t.rearrange("(j p) t -> p j t", p=128),
                              in_=xs.rearrange("p (j t) -> p j t", j=NJ)),
                  "qout", 16)
        block.gpsimd(dbg_gy)

    if DEBUG_DUMP == "bx8":
        def dbg_g8(g):
            g.wait_ge(p.sem("bds"), 1)
            p.inc(g.dma_start(out=out_t.rearrange("(j p) t -> p j t", p=128),
                              in_=xs.rearrange("p (j t) -> p j t", j=NJ)),
                  "qout", 16)
        block.gpsimd(dbg_g8)

    def b8_vector(vector):
        for c in range(NCH):
            for j in range(NJ):
                k = c * NJ + j
                if DEBUG_DUMP == "bx8":
                    vector.wait_ge(p.sem("qout"), 16)
                vector.wait_ge(p.sem("bds"), k + 1)
                if DEBUG_DUMP == "bdown":
                    vector.tensor_copy(aux[:, j * M + c * CSZ:
                                           j * M + c * CSZ + 512],
                                       ch2[:, 0:512])
                    vector.drain()
                vector.tensor_tensor(xsl(j, c), xsl(j, c), ch2[:, 0:512],
                                     OP.add)
                p.inc(vector.drain(), "bde")
    block.vector(b8_vector)

    nmB2 = emit_ln("lB2", fused_gamma=False, pre=[("bde", NCH * NJ)])

    NSS = 16

    xxt = [ch1[:, 0:M], ch2[:, 0:M], hbuf[:, 0:M], brep[:, 0:M], crep[:, 0:M],
           wb[:, 0:2048], wb[:, 2048:4096], wb[:, 4096:6144]]

    def dtsl(j2):
        base = ch1 if j2 < 4 else ch2
        return base[:, (j2 % 4) * 512:(j2 % 4) * 512 + 512]

    def ztsl(j2):
        base = ch1 if j2 < 4 else ch2
        return base[:, (j2 % 4) * 512:(j2 % 4) * 512 + 512]

    def ytsl(j2):
        base = hbuf if j2 < 4 else brep
        return base[:, (j2 % 4) * 512:(j2 % 4) * 512 + 512]

    def cdtl(j2):
        if j2 < 4:
            return crep[:, j2 * 512: j2 * 512 + 512]
        return wb[:, (j2 - 4) * 512:(j2 - 4) * 512 + 512]

    def xcc(j2):
        return wb[:, 2048 + j2 * 512: 2048 + j2 * 512 + 512]

    sc_br = aux[:, 0:512]
    sc_cr = aux[:, 512:1024]
    sc_dA = [aux[:, 1024:1536], aux[:, 2560:3072]]
    sc_dB = [aux[:, 1536:2048], aux[:, 3072:3584]]
    sc_H = [aux[:, 2048:2560], aux[:, 3584:4096]]
    onesT = aux[:, 4096:4608]
    sc_t1 = aux[:, 4608:5120]
    xcs = [aux[:, 5120:7168], aux[:, 7168:9216]]
    states = aux[:, 9216:9344]
    ccar = aux[:, 9344:9352]
    a_cols = aux[:, 9352:9480]
    d_cols = aux[:, 9480:9488]
    bdt_c = aux[:, 9488:9496]
    cw_c = aux[:, 9496:9520]
    hx1 = aux[:, 9520:9536]
    hx2 = aux[:, 9536:9552]
    hxe = aux[:, 9552:9568]
    wxs = aux[:, 9568:9824]
    wsl = [wb[:, 6144:7168], wb[:, 7168:8192]]
    g3s = pack1[:, 1294:1302]
    b3s = pack1[:, 1302:1310]
    zst = [pack1[:, 0:512], pack1[:, 512:1024]]
    browB = strow[0:1, 0:512]
    browC = strow[0:1, 512:1024]

    def cg_sync(sync):
        p.inc(sync.dma_start(out=g3s, in_=g3ch), "qw", 16)
        p.inc(sync.dma_start(out=b3s, in_=b3ch), "qw", 16)
    block.sync(cg_sync)
    QW_G3 = p.v("qw")

    nmCi = emit_ln("lCi", pre=[nmB2, ("qw", QW_G3)],
                   affine=(g3s, b3s))

    def c2_sync(sync):
        sync.wait_ge(p.sem("bdm"), NCH * NI)
        for i in range(16):
            if i >= 2:
                sync.wait_ge(p.sem("cm"), 4 * (i - 1))
            p.inc(sync.dma_start(
                out=wsl[i % 2].rearrange("p (j e) -> p j e", j=NJ),
                in_=w_in_t[:, i * 128:(i + 1) * 128]
                    .rearrange("(j p) e -> p j e", p=128)), "qw", 16)
    block.sync(c2_sync)
    QW_C2 = p.v("qw") - 16 * 16

    def c2_tensor(tensor):
        tensor.wait_ge(p.sem(nmCi[0]), nmCi[1])
        for i in range(16):
            tensor.wait_ge(p.sem("qw"), QW_C2 + 16 * (i + 1))
            for c in range(NCH):
                k = i * NCH + c
                if k >= 4:
                    tensor.wait_ge(p.sem("cev"), k - 3)
                for j in range(NJ):
                    mm = tensor.matmul(pb[k % 4][:, :],
                                       wsl[i % 2][:, j * 128:(j + 1) * 128],
                                       auxl(j, c),
                                       start=(j == 0), stop=(j == NJ - 1))
                p.inc(mm, "cm")
    block.tensor(c2_tensor)

    QH_C2 = p.v("qh")

    def c2_scalar(scalar):
        for i in range(16):
            for c in range(NCH):
                k = i * NCH + c
                scalar.wait_ge(p.sem("cm"), k + 1)
                if i < 8:
                    scalar.activation(out=xxt[i][:, c * 512:c * 512 + 512],
                                      in_=pb[k % 4][:, :], func=AF.Copy)
                else:
                    zi = (i - 8) * NCH + c
                    if zi >= 2:
                        scalar.wait_ge(p.sem("qh"), QH_C2 + 16 * (zi - 1))
                    scalar.activation(out=zst[zi % 2], in_=pb[k % 4][:, :],
                                      func=AF.Copy)
                p.inc(scalar.drain(), "cev")
    block.scalar(c2_scalar)

    def c2_gpsimd(g):
        for zi in range(32):
            i, c = 8 + zi // NCH, zi % NCH
            g.wait_ge(p.sem("cev"), i * NCH + c + 1)
            p.inc(g.dma_start(out=z_dram[i - 8][:, c * 512:c * 512 + 512],
                              in_=zst[zi % 2]), "qh", 16)
    block.gpsimd(c2_gpsimd)

    def c0_sync(sync):
        sync.wait_ge(p.sem("cm"), 64)
        for src_, dst in ((a_log_c, a_cols), (d_colh, d_cols),
                          (bdt_colh, bdt_c), (cw_colh, cw_c)):
            p.inc(sync.dma_start(out=dst, in_=src_), "qw", 16)
        p.inc(sync.dma_start(
            out=wxs.rearrange("p (j s) -> p j s", j=NJ),
            in_=w_x_t.rearrange("(j p) s -> p j s", p=128)), "qw", 16)
    block.sync(c0_sync)
    QW_CC = p.v("qw")

    def c0_scalar(scalar):
        scalar.wait_ge(p.sem("qw"), QW_CC)
        scalar.activation(out=a_cols, in_=a_cols, func=AF.Exp)
        p.inc(scalar.drain(), "bs")
    block.scalar(c0_scalar)
    BS_C0 = p.v("bs")

    def c0_vector(vector):
        vector.wait_ge(p.sem("bs"), BS_C0)
        vector.tensor_scalar(a_cols, a_cols, -1.0, None, OP.mult)
        vector.drain()
        vector.tensor_scalar(states, onsq[:, 0:128], 0.0, None, OP.mult)
        vector.drain()
        vector.tensor_scalar(ccar, onsq[:, 0:8], 0.0, None, OP.mult)
        vector.drain()
        for q in range(4):
            vector.tensor_scalar(onesT[:, q * 128:(q + 1) * 128],
                                 onsq[:, 0:128], 1.0, None, OP.mult)
        p.inc(vector.drain(), "bv")
    block.vector(c0_vector)
    BV_C0 = p.v("bv")

    def c3_vector(vector):
        vector.wait_ge(p.sem("cev"), 64)
        vector.wait_ge(p.sem("bv"), BV_C0)
        for j2 in range(NJ):
            vector.scalar_tensor_tensor(hx1[:, j2:j2 + 1], xxt[j2][:, 0:1],
                                        mko[:, 0:1], xxt[j2][:, 0:1],
                                        OP.mult, OP.bypass)
            vector.scalar_tensor_tensor(hx1[:, 8 + j2:9 + j2],
                                        xxt[j2][:, 2047:2048], mke[:, 0:1],
                                        xxt[j2][:, 0:1], OP.mult, OP.bypass)
        p.inc(vector.drain(), "bv")
    block.vector(c3_vector)
    BV_HX = p.v("bv")

    def c3_g(g):
        g.wait_ge(p.sem("bv"), BV_HX)
        p.inc(g.dma_start(out=hx_in, in_=hx1), "qg", 16)
        g.wait_ge(p.sem("qg"), p.v("qg"))
        p.inc(g.collective_compute("AllReduce", OP.add,
                                   replica_groups=[[0, 1], [2, 3], [4, 5],
                                                   [6, 7]],
                                   ins=[hx_in], outs=[hx_out]), "cc")
        g.wait_ge(p.sem("cc"), p.v("cc"))
        p.inc(g.dma_start(out=hx2, in_=hx_out), "qg", 16)
    block.gpsimd(c3_g)
    QG_HX = p.v("qg")

    QH_XC = p.v("qh")
    BV_CCV = p.v("bv")

    def c3v2(vector):
        vector.wait_ge(p.sem("qg"), QG_HX)
        for j2 in range(NJ):
            vector.scalar_tensor_tensor(hxe[:, j2:j2 + 1],
                                        hx2[:, 8 + j2:9 + j2], mko[:, 0:1],
                                        hx2[:, j2:j2 + 1], OP.mult, OP.bypass)
            vector.scalar_tensor_tensor(hxe[:, 8 + j2:9 + j2],
                                        hx2[:, j2:j2 + 1], mke[:, 0:1],
                                        hx2[:, j2:j2 + 1], OP.mult, OP.bypass)
        vector.drain()
        for j2 in range(NJ):
            if j2 >= 2:
                vector.wait_ge(p.sem("qh"), QH_XC + 16 * (j2 - 1))
            x_ = xxt[j2]
            xc_ = xcs[j2 % 2]
            vector.scalar_tensor_tensor(xc_, x_,
                                        cw_c[:, 3 * j2 + 1:3 * j2 + 2],
                                        x_, OP.mult, OP.bypass)
            vector.drain()
            vector.scalar_tensor_tensor(xc_[:, 1:2048], x_[:, 0:2047],
                                        cw_c[:, 3 * j2:3 * j2 + 1],
                                        xc_[:, 1:2048], OP.mult, OP.add)
            vector.drain()
            vector.scalar_tensor_tensor(xc_[:, 0:2047], x_[:, 1:2048],
                                        cw_c[:, 3 * j2 + 2:3 * j2 + 3],
                                        xc_[:, 0:2047], OP.mult, OP.add)
            vector.drain()
            vector.scalar_tensor_tensor(xc_[:, 0:1], hxe[:, j2:j2 + 1],
                                        cw_c[:, 3 * j2:3 * j2 + 1],
                                        xc_[:, 0:1], OP.mult, OP.add)
            vector.scalar_tensor_tensor(xc_[:, 2047:2048],
                                        hxe[:, 8 + j2:9 + j2],
                                        cw_c[:, 3 * j2 + 2:3 * j2 + 3],
                                        xc_[:, 2047:2048], OP.mult, OP.add)
            p.inc(vector.drain(), "bv")
    block.vector(c3v2)

    BS_SL = p.v("bs")

    def c3_scalar(scalar):
        for j2 in range(NJ):
            scalar.wait_ge(p.sem("bv"), BV_CCV + j2 + 1)
            scalar.activation(out=xcs[j2 % 2], in_=xcs[j2 % 2], func=AF.Silu)
            p.inc(scalar.drain(), "bs")
    block.scalar(c3_scalar)

    def c3_g2(g):
        for j2 in range(NJ):
            g.wait_ge(p.sem("bs"), BS_SL + j2 + 1)
            p.inc(g.dma_start(out=xc_dram[j2], in_=xcs[j2 % 2]), "qh", 16)
    block.gpsimd(c3_g2)

    QW_C4 = p.v("qw")
    QG_XL = p.v("qg")

    def c4_sync(sync):
        for c in range(NCH):
            for e2 in range(NJ):
                k = c * NJ + e2
                if k >= 2:
                    sync.wait_ge(p.sem("c4m"), k - 1)
                p.inc(sync.dma_start(
                    out=wsl[k % 2].rearrange("p (j e) -> p j e", j=NJ),
                    in_=w_dt_t[:, e2 * 128:(e2 + 1) * 128]
                        .rearrange("(j p) e -> p j e", p=128)), "qw", 16)
    block.sync(c4_sync)

    def c4_tensor(tensor):
        tensor.wait_ge(p.sem("qw"), QW_CC)
        tensor.wait_ge(p.sem("cev"), 64)
        for c in range(NCH):
            for e2 in range(NJ):
                k = c * NJ + e2
                tensor.wait_ge(p.sem("qw"), QW_C4 + 16 * (k + 1))
                tensor.wait_ge(p.sem("qg"), QG_XL + 128 * (c + 1))
                if k >= 2:
                    tensor.wait_ge(p.sem("c4e"), k - 1)
                for j in range(NJ):
                    mm = tensor.matmul(pb[k % 2][:, :],
                                       wsl[k % 2][:, j * 128:(j + 1) * 128],
                                       xcc(j),
                                       start=(j == 0), stop=(j == NJ - 1))
                p.inc(mm, "c4m")
            if c >= 1:
                tensor.wait_ge(p.sem("cxe"), c)
            for j in range(NJ):
                mm = tensor.matmul(pb[2][0:32, :], wxs[:, j * 32:(j + 1) * 32],
                                   xcc(j),
                                   start=(j == 0), stop=(j == NJ - 1))
            p.inc(mm, "cm")
            for s in range(NSS):
                gs = c * NSS + s
                tensor.wait_ge(p.sem("crd"), (gs + 1) * 32)
                if gs >= 1:
                    tensor.wait_ge(p.sem("cre"), 2 * gs)
                tensor.matmul(pb[3][:, :], onr[:], browB, start=True,
                              stop=True)
                p.inc(tensor.matmul(pb[4][:, :], onr[:], browC, start=True,
                                    stop=True), "crm")
    block.tensor(c4_tensor)

    QH_P1 = p.v("qh")

    def c4_scalar(scalar):
        for c in range(NCH):
            for e2 in range(NJ):
                k = c * NJ + e2
                scalar.wait_ge(p.sem("c4m"), k + 1)
                if c >= 1:
                    scalar.wait_ge(p.sem("csc"), c * 128)
                scalar.activation(out=dtsl(e2), in_=pb[k % 2][:, :],
                                  func=AF.Copy)
                p.inc(scalar.drain(), "c4e")
            for e2 in range(NJ):
                g_ = c * NJ + e2
                par = e2 % 2
                scalar.wait_ge(p.sem("csp"), g_ + 1)
                if g_ >= 2:
                    scalar.wait_ge(p.sem("c4s"), g_ - 1)
                scalar.activation(out=sc_dB[par], in_=sc_dB[par], func=AF.Exp)
                scalar.drain()
                scalar.activation(out=sc_dB[par], in_=sc_dB[par], func=AF.Ln,
                                  bias=cst[:, 4:5])
                scalar.drain()
                scalar.activation(out=sc_H[par], in_=dtsl(e2), func=AF.Relu)
                p.inc(scalar.drain(), "csq")
            scalar.wait_ge(p.sem("cm"), 64 + c + 1)
            if c >= 1:
                scalar.wait_ge(p.sem("qh"), QH_P1 + 272 * c - 256)
            scalar.activation(out=sc_t1[0:32, 0:512], in_=pb[2][0:32, :],
                              func=AF.Copy)
            p.inc(scalar.drain(), "cxe")
            for s in range(NSS):
                gs = c * NSS + s
                scalar.wait_ge(p.sem("crm"), gs + 1)
                if gs >= 1:
                    scalar.wait_ge(p.sem("csc"), c * 128 + s * 8)
                scalar.activation(out=sc_br, in_=pb[3][:, :], func=AF.Copy)
                p.inc(scalar.drain(), "cre")
                scalar.activation(out=sc_cr, in_=pb[4][:, :], func=AF.Copy)
                p.inc(scalar.drain(), "cre")
                for j2 in range(NJ):
                    step = c * 128 + s * 8 + j2
                    par = step % 2
                    scalar.wait_ge(p.sem("c4s"), c * NJ + NJ)
                    if step >= 2:
                        scalar.wait_ge(p.sem("csc"), step - 1)
                    scalar.activation(
                        out=sc_dA[par], in_=dtsl(j2), func=AF.Exp,
                        scale=a_cols[:, j2 * 16 + s:j2 * 16 + s + 1])
                    p.inc(scalar.drain(), "cde")
    block.scalar(c4_scalar)

    def c4_vector(vector):
        for c in range(NCH):
            vector.wait_ge(p.sem("qh"), QH_P1 + 272 * c)
            for e2 in range(NJ):
                g_ = c * NJ + e2
                par = e2 % 2
                vector.wait_ge(p.sem("c4e"), g_ + 1)
                if g_ >= 2:
                    vector.wait_ge(p.sem("csq"), g_ - 1)
                vector.scalar_tensor_tensor(dtsl(e2), dtsl(e2),
                                            bdt_c[:, e2:e2 + 1], dtsl(e2),
                                            OP.add, OP.bypass)
                vector.drain()
                vector.tensor_scalar(sc_dB[par], dtsl(e2), -1.0, None,
                                     OP.mult)
                vector.drain()
                vector.tensor_tensor(sc_dB[par], sc_dB[par], dtsl(e2), OP.min)
                p.inc(vector.drain(), "csp")
                if e2 >= 1:
                    ep = e2 - 1
                    vector.wait_ge(p.sem("csq"), c * NJ + ep + 1)
                    vector.tensor_tensor(dtsl(ep), sc_H[ep % 2],
                                         sc_dB[ep % 2], OP.add)
                    p.inc(vector.drain(), "c4s")
            vector.wait_ge(p.sem("csq"), c * NJ + NJ)
            vector.tensor_tensor(dtsl(NJ - 1), sc_H[(NJ - 1) % 2],
                                 sc_dB[(NJ - 1) % 2], OP.add)
            p.inc(vector.drain(), "c4s")
            for j2 in range(NJ):
                vector.tensor_tensor_scan(cdtl(j2), onesT, dtsl(j2),
                                          ccar[:, j2:j2 + 1], OP.mult, OP.add)
                vector.drain()
                vector.tensor_copy(ccar[:, j2:j2 + 1], cdtl(j2)[:, 511:512])
                p.inc(vector.drain(), "ccd")
            for s in range(NSS):
                for j2 in range(NJ):
                    step = c * 128 + s * 8 + j2
                    par = step % 2
                    idx = j2 * 16 + s
                    vector.wait_ge(p.sem("cde"), step + 1)
                    vector.wait_ge(p.sem("cre"), 2 * (c * NSS + s) + 2)
                    vector.tensor_tensor(sc_dB[par], dtsl(j2), sc_br, OP.mult)
                    vector.drain()
                    vector.tensor_tensor_scan(sc_H[par], sc_dA[par],
                                              sc_dB[par],
                                              states[:, idx:idx + 1],
                                              OP.mult, OP.add)
                    vector.drain()
                    vector.tensor_copy(states[:, idx:idx + 1],
                                       sc_H[par][:, 511:512])
                    vector.drain()
                    if s == 0:
                        vector.tensor_tensor(ytsl(j2), sc_H[par], sc_cr,
                                             OP.mult)
                    else:
                        vector.tensor_tensor(sc_dA[par], sc_H[par], sc_cr,
                                             OP.mult)
                        vector.drain()
                        vector.tensor_tensor(ytsl(j2), ytsl(j2), sc_dA[par],
                                             OP.add)
                    p.inc(vector.drain(), "csc")
    block.vector(c4_vector)

    def c4_gpsimd(g):
        for c in range(NCH):
            g.wait_ge(p.sem("bv"), BV_CCV + NJ)
            if c >= 1:
                g.wait_ge(p.sem("c4m"), c * NJ)
                g.wait_ge(p.sem("cm"), 64 + c)
            for j2 in range(NJ):
                p.inc(g.dma_start(out=xcc(j2),
                                  in_=xc_dram[j2][:, c * 512:c * 512 + 512]),
                      "qg", 16)
            for s in range(NSS):
                gs = c * NSS + s
                g.wait_ge(p.sem("cxe"), c + 1)
                g.wait_ge(p.sem("crm"), gs)
                p.inc(g.dma_start(out=browB, in_=sc_t1[s:s + 1, 0:512]),
                      "crd", 16)
                p.inc(g.dma_start(out=browC,
                                  in_=sc_t1[16 + s:17 + s, 0:512]),
                      "crd", 16)
            p.inc(g.dma_start(out=bc_dram[:, c * 512:c * 512 + 512],
                              in_=sc_t1[0:32, 0:512]), "qh", 16)
            for j2 in range(NJ):
                g.wait_ge(p.sem("csc"), c * 128 + 15 * 8 + j2 + 1)
                p.inc(g.dma_start(out=y_dram[j2][:, c * 512:c * 512 + 512],
                                  in_=ytsl(j2)), "qh", 16)
            for j2 in range(NJ):
                g.wait_ge(p.sem("ccd"), c * NJ + j2 + 1)
                p.inc(g.dma_start(out=cdt_dram[j2][:, c * 512:c * 512 + 512],
                                  in_=cdtl(j2)), "qh", 16)
    block.gpsimd(c4_gpsimd)

    def cx_vector(vector):
        vector.wait_ge(p.sem("csc"), 512)
        vector.scalar_tensor_tensor(sc_dA[0][:, 0:128], states, mke[:, 0:1],
                                    states, OP.mult, OP.bypass)
        p.inc(vector.drain(), "bv")
    block.vector(cx_vector)
    BV_STX = p.v("bv")

    def cx_g(g):
        g.wait_ge(p.sem("bv"), BV_STX)
        p.inc(g.dma_start(out=st_in, in_=sc_dA[0][:, 0:128]), "qg", 16)
        g.wait_ge(p.sem("qg"), p.v("qg"))
        p.inc(g.collective_compute("AllReduce", OP.add,
                                   replica_groups=[[0, 1], [2, 3], [4, 5],
                                                   [6, 7]],
                                   ins=[st_in], outs=[st_out]), "cc")
        g.wait_ge(p.sem("cc"), p.v("cc"))
        p.inc(g.dma_start(out=states, in_=st_out), "qg", 16)
    block.gpsimd(cx_g)
    QG_STX = p.v("qg")

    def cx_v2(vector):
        vector.wait_ge(p.sem("qg"), QG_STX)
        vector.scalar_tensor_tensor(states, states, mko[:, 0:1], states,
                                    OP.mult, OP.bypass)
        p.inc(vector.drain(), "bv")
    block.vector(cx_v2)
    BV_CSTX = p.v("bv")

    QW_P2 = p.v("qw")
    CRD_P2 = p.v("crd")
    CRM_P2 = p.v("crm")
    CRE_P2 = p.v("cre")
    CDE_P2 = p.v("cde")
    QH_P2 = p.v("qh")
    CXL_P2 = p.v("qg")

    def p2_sync(sync):
        sync.wait_ge(p.sem("c4m"), 32)
        for c in range(NCH):
            for e in range(NJ):
                k = c * NJ + e
                if k >= 2:
                    sync.wait_ge(p.sem("p2m"), k - 1)
                p.inc(sync.dma_start(
                    out=wsl[k % 2].rearrange("p (j e) -> p j e", j=NJ),
                    in_=w_out_t[:, e * 128:(e + 1) * 128]
                        .rearrange("(j p) e -> p j e", p=128)), "qw", 16)
    block.sync(p2_sync)

    def p2_g(g):
        g.wait_ge(p.sem("csc"), 512)
        for c in range(NCH):
            for j2 in range(NJ):
                if c >= 1:
                    g.wait_ge(p.sem("p2m"), c * NJ)
                    g.wait_ge(p.sem("p2k"), c * 128)
                    g.wait_ge(p.sem("p2s"), c * NJ)
                p.inc(g.dma_start(out=ytsl(j2),
                                  in_=y_dram[j2][:, c * 512:c * 512 + 512]),
                      "qh", 16)
                p.inc(g.dma_start(out=cdtl(j2),
                                  in_=cdt_dram[j2][:, c * 512:c * 512 + 512]),
                      "qh", 16)
                p.inc(g.dma_start(out=ztsl(j2),
                                  in_=z_dram[j2][:, c * 512:c * 512 + 512]),
                      "qh", 16)
                p.inc(g.dma_start(out=xcc(j2),
                                  in_=xc_dram[j2][:, c * 512:c * 512 + 512]),
                      "qg", 16)
            for s in range(NSS):
                g.wait_ge(p.sem("crm"), CRM_P2 + c * NSS + s)
                p.inc(g.dma_start(
                    out=browC,
                    in_=bc_dram[16 + s:17 + s, c * 512:c * 512 + 512]),
                    "crd", 16)
    block.gpsimd(p2_g)

    def p2_tensor(tensor):
        for c in range(NCH):
            for s in range(NSS):
                gs = c * NSS + s
                tensor.wait_ge(p.sem("crd"), CRD_P2 + (gs + 1) * 16)
                tensor.wait_ge(p.sem("cre"), CRE_P2 + gs)
                p.inc(tensor.matmul(pb[4][:, :], onr[:], browC, start=True,
                                    stop=True), "crm")
            for e in range(NJ):
                k = c * NJ + e
                tensor.wait_ge(p.sem("qw"), QW_P2 + 16 * (k + 1))
                tensor.wait_ge(p.sem("p2s"), (c + 1) * NJ)
                if k >= 2:
                    tensor.wait_ge(p.sem("p2e"), k - 1)
                for j2 in range(NJ):
                    mm = tensor.matmul(pb[5 + k % 2][:, :],
                                       wsl[k % 2][:, j2 * 128:(j2 + 1) * 128],
                                       ytsl(j2),
                                       start=(j2 == 0), stop=(j2 == NJ - 1))
                p.inc(mm, "p2m")
    block.tensor(p2_tensor)

    BS_P2L = p.v("bs")

    def p2_scalar(scalar):
        scalar.wait_ge(p.sem("csc"), 512)
        for c in range(NCH):
            for s in range(NSS):
                gs = c * NSS + s
                scalar.wait_ge(p.sem("crm"), CRM_P2 + gs + 1)
                if gs >= 1:
                    scalar.wait_ge(p.sem("p2k"), gs * 8)
                scalar.activation(out=sc_cr, in_=pb[4][:, :], func=AF.Copy)
                p.inc(scalar.drain(), "cre")
                for j2 in range(NJ):
                    step = c * 128 + s * 8 + j2
                    par = step % 2
                    if step >= 2:
                        scalar.wait_ge(p.sem("p2k"), step - 1)
                    scalar.wait_ge(p.sem("qh"),
                                   QH_P2 + 16 * (c * 24 + j2 * 3 + 2))
                    scalar.activation(
                        out=sc_dA[par], in_=cdtl(j2), func=AF.Exp,
                        scale=a_cols[:, j2 * 16 + s:j2 * 16 + s + 1])
                    p.inc(scalar.drain(), "cde")
            for j2 in range(NJ):
                scalar.wait_ge(p.sem("qh"),
                               QH_P2 + 16 * (c * 24 + j2 * 3 + 3))
                scalar.activation(out=ztsl(j2), in_=ztsl(j2), func=AF.Silu)
                p.inc(scalar.drain(), "bs")
            for e in range(NJ):
                k = c * NJ + e
                scalar.wait_ge(p.sem("p2m"), k + 1)
                if k >= 2:
                    scalar.wait_ge(p.sem("p2x"), k - 1)
                scalar.activation(out=sc_H[k % 2], in_=pb[5 + k % 2][:, :],
                                  func=AF.Copy)
                p.inc(scalar.drain(), "p2e")
    block.scalar(p2_scalar)

    def p2_vector(vector):
        vector.wait_ge(p.sem("bv"), BV_CSTX)
        for c in range(NCH):
            for s in range(NSS):
                for j2 in range(NJ):
                    step = c * 128 + s * 8 + j2
                    par = step % 2
                    idx = j2 * 16 + s
                    vector.wait_ge(p.sem("cde"), CDE_P2 + step + 1)
                    vector.wait_ge(p.sem("cre"), CRE_P2 + c * NSS + s + 1)
                    if s == 0:
                        vector.wait_ge(p.sem("qh"),
                                       QH_P2 + 16 * (c * 24 + j2 * 3 + 1))
                    vector.scalar_tensor_tensor(sc_dB[par], sc_dA[par],
                                                states[:, idx:idx + 1],
                                                sc_cr, OP.mult, OP.mult)
                    vector.drain()
                    vector.tensor_tensor(ytsl(j2), ytsl(j2), sc_dB[par],
                                         OP.add)
                    p.inc(vector.drain(), "p2k")
            for j2 in range(NJ):
                vector.wait_ge(p.sem("qg"), CXL_P2 + 16 * (c * NJ + j2 + 1))
                vector.scalar_tensor_tensor(ytsl(j2), xcc(j2),
                                            d_cols[:, j2:j2 + 1], ytsl(j2),
                                            OP.mult, OP.add)
                vector.drain()
                vector.wait_ge(p.sem("bs"), BS_P2L + c * NJ + j2 + 1)
                vector.tensor_tensor(ytsl(j2), ytsl(j2), ztsl(j2), OP.mult)
                p.inc(vector.drain(), "p2s")
            for e in range(NJ):
                k = c * NJ + e
                vector.wait_ge(p.sem("p2e"), k + 1)
                vector.scalar_tensor_tensor(xsl(e, c), xsl(e, c), cst[:, 3:4],
                                            sc_H[k % 2], OP.mult, OP.add)
                p.inc(vector.drain(), "p2x")
    block.vector(p2_vector)

    nmC = emit_ln("lC", pre=[("p2x", NCH * NJ)])

    hxD1 = pack1[:, 1024:1088]
    cwD = pack1[:, 1088:1184]
    cbD = pack1[:, 1184:1216]
    idf = pack1[:, 1216:1344]
    NID = 32

    def dd_sync(sync):
        sync.wait_ge(p.sem("p2x"), NCH * NJ)
        p.inc(sync.dma_start(out=cwD, in_=cwD_h), "qw", 16)
        p.inc(sync.dma_start(out=cbD, in_=cbD_h), "qw", 16)
        p.inc(sync.dma_start(out=idf, in_=idf_h), "qw", 16)
    block.sync(dd_sync)
    QW_DD = p.v("qw")

    BV_D0 = p.v("bv")

    def d0_v(vector):
        vector.wait_ge(p.sem(nmC[0]), nmC[1])
        vector.wait_ge(p.sem("qw"), QW_DD)
        for j in range(NJ):
            vector.tensor_scalar(ch2[:, 0:M], xsl(j), -1.0, None, OP.mult)
            vector.drain()
            vector.tensor_tensor(ch2[:, 0:M], ch2[:, 0:M], xsl(j), OP.max)
            vector.drain()
            if j == 0:
                vector.tensor_copy(hbuf[:, 0:M], ch2[:, 0:M])
            else:
                vector.tensor_tensor(hbuf[:, 0:M], hbuf[:, 0:M], ch2[:, 0:M],
                                     OP.max)
            p.inc(vector.drain(), "bv")
    block.vector(d0_v)
    BV_M8 = p.v("bv")

    def d0_t(tensor):
        tensor.wait_ge(p.sem("bv"), BV_M8)
        for tb in range(16):
            if tb >= 4:
                tensor.wait_ge(p.sem("bv"), BV_M8 + tb - 3)
            p.inc(tensor.transpose(pb[4 + tb % 4][:, 0:128],
                                   hbuf[:, tb * 128:(tb + 1) * 128],
                                   idf[:, :]), "dtr")
    block.tensor(d0_t)

    def d0_v2(vector):
        for tb in range(16):
            vector.wait_ge(p.sem("dtr"), tb + 1)
            vector.reduce_max(sml[:, tb:tb + 1], pb[4 + tb % 4][:, 0:128],
                              axis=AX)
            p.inc(vector.drain(), "bv")
        vector.tensor_scalar(sml[:, 0:16], sml[:, 0:16], 1e-5, None, OP.max)
        p.inc(vector.drain(), "bv")
    block.vector(d0_v2)
    BV_D0C = p.v("bv")

    QG_D0 = p.v("qg")

    def d0_g(g):
        g.wait_ge(p.sem("bv"), BV_D0C)
        for tb in range(16):
            p.inc(g.dma_start(out=crep[0:1, tb * 128:(tb + 1) * 128],
                              in_=sml[:, tb:tb + 1]), "qg", 16)
    block.gpsimd(d0_g)
    QG_D0R = p.v("qg")

    def d0_v3(vector):
        vector.wait_ge(p.sem("qg"), QG_D0R)
        vector.reciprocal(crep[0:1, 0:M], crep[0:1, 0:M])
        vector.drain()
        vector.tensor_scalar(crep[0:1, 0:M], crep[0:1, 0:M], 127.0, None,
                             OP.mult)
        p.inc(vector.drain(), "bv")
    block.vector(d0_v3)
    BV_SR = p.v("bv")

    def d0_t2(tensor):
        tensor.wait_ge(p.sem("bv"), BV_SR)
        for c in range(NCH):
            p.inc(tensor.matmul(pb[c][:, :], onr[:],
                                crep[0:1, c * 512:c * 512 + 512],
                                start=True, stop=True), "dm")
    block.tensor(d0_t2)

    def d0_s(scalar):
        for c in range(NCH):
            scalar.wait_ge(p.sem("dm"), c + 1)
            scalar.activation(out=crep[:, c * 512:c * 512 + 512],
                              in_=pb[c][:, :], func=AF.Copy)
            p.inc(scalar.drain(), "dev")
    block.scalar(d0_s)

    def d0_v4(vector):
        vector.wait_ge(p.sem("dev"), NCH)
        for j in range(NJ):
            vector.tensor_tensor(ich2[:, :], xsl(j), crep[:, 0:M], OP.mult)
            vector.drain()
            vector.tensor_scalar(xqb[:, j * M:(j + 1) * M], ich2[:, :],
                                 -128.0, 127.0, OP.max, OP.min)
            p.inc(vector.drain(), "bv")
    block.vector(d0_v4)
    BV_XQ = p.v("bv")

    QW_D1 = p.v("qw")

    BV_D1A0 = p.v("bv")

    def d1_syncA(sync):
        sync.wait_ge(p.sem(nmC[0]), nmC[1])
        for kq in range(32):
            if kq >= 2:
                sync.wait_ge(p.sem("bv"), BV_D1A0 + kq - 1)
            j, q = kq // 4, kq % 4
            p.inc(sync.dma_start(
                out=ch1[:, (kq % 2) * 1024:(kq % 2) * 1024 + 1024],
                in_=w1_th[j * 128:(j + 1) * 128,
                          q * 1024:(q + 1) * 1024]), "qw", 16)
    block.sync(d1_syncA)

    def d1_vA(vector):
        for kq in range(32):
            vector.wait_ge(p.sem("qw"), QW_D1 + 16 * (kq + 1))
            vector.reduce_sum(sml[:, 16:17],
                              ch1[:, (kq % 2) * 1024:(kq % 2) * 1024 + 1024],
                              axis=AX, apply_absolute_value=True)
            vector.drain()
            if kq == 0:
                vector.tensor_copy(sml[:, 18:19], sml[:, 16:17])
            else:
                vector.tensor_tensor(sml[:, 18:19], sml[:, 18:19],
                                     sml[:, 16:17], OP.add)
            p.inc(vector.drain(), "bv")
    block.vector(d1_vA)
    BV_W1S = p.v("bv")

    def d1_cell(tensor):
        tensor.wait_ge(p.sem("bv"), BV_W1S)
        tensor.wait_ge(p.sem("dev"), NCH)
        p.inc(tensor.matmul(pb[0][0:1, 0:1], sml[:, 18:19], onsq[:, 0:1],
                            start=True, stop=True), "dm")
    block.tensor(d1_cell)
    DM_W1 = p.v("dm")

    def d1_cs(scalar):
        scalar.wait_ge(p.sem("dm"), DM_W1)
        scalar.copy(out=sml[0:1, 48:49], in_=pb[0][0:1, 0:1])
        p.inc(scalar.drain(), "bs")
    block.scalar(d1_cs)
    BS_W1C = p.v("bs")

    def d1_cv(vector):
        vector.wait_ge(p.sem("bs"), BS_W1C)
        vector.tensor_scalar(sml[0:1, 49:50], sml[0:1, 48:49],
                             1.0 / 4194304.0, None, OP.mult)
        vector.drain()
        vector.tensor_scalar(sml[0:1, 50:51], sml[0:1, 49:50],
                             1e-5, None, OP.max)
        vector.drain()
        vector.reciprocal(sml[0:1, 51:52], sml[0:1, 50:51])
        p.inc(vector.drain(), "bv")
    block.vector(d1_cv)
    BV_W1C = p.v("bv")

    def d1_ct(tensor):
        tensor.wait_ge(p.sem("bv"), BV_W1C)
        p.inc(tensor.matmul(pb[1][:, 0:1], onr[:], sml[0:1, 51:52],
                            start=True, stop=True), "dm")
    block.tensor(d1_ct)
    DM_W1R = p.v("dm")

    def d1_cs2(scalar):
        scalar.wait_ge(p.sem("dm"), DM_W1R)
        scalar.copy(out=qsc[:, 0:1], in_=pb[1][:, 0:1])
        p.inc(scalar.drain(), "bs")
    block.scalar(d1_cs2)
    BS_W1R = p.v("bs")

    QW_D1B = p.v("qw")
    QH_D1B = p.v("qh")

    def d1_syncB(sync):
        sync.wait_ge(p.sem("bs"), BS_W1R)
        sync.wait_ge(p.sem("bv"), BV_XQ)
        for kq in range(32):
            if kq >= 2:
                sync.wait_ge(p.sem("dq2"), kq - 1)
            j, q = kq // 4, kq % 4
            p.inc(sync.dma_start(
                out=ch1[:, (kq % 2) * 1024:(kq % 2) * 1024 + 1024],
                in_=w1_th[j * 128:(j + 1) * 128,
                          q * 1024:(q + 1) * 1024]), "qw", 16)
    block.sync(d1_syncB)

    def d1_vB(vector):
        for kq in range(32):
            j, q = kq // 4, kq % 4
            vector.wait_ge(p.sem("qw"), QW_D1B + 16 * (kq + 1))
            if q == 0 and j >= 1:
                vector.wait_ge(p.sem("qh"), QH_D1B + 16 * j)
            sl1 = slice((kq % 2) * 1024, (kq % 2) * 1024 + 1024)
            vector.scalar_tensor_tensor(ich2[:, sl1], ch1[:, sl1],
                                        qsc[:, 0:1], ch1[:, sl1],
                                        OP.mult, OP.bypass)
            vector.drain()
            vector.tensor_scalar(dqw[:, q * 1024:(q + 1) * 1024],
                                 ich2[:, sl1], -1.0, 1.0, OP.max, OP.min)
            p.inc(vector.drain(), "dq2")
    block.vector(d1_vB)

    def d1_gB(g):
        for j in range(NJ):
            g.wait_ge(p.sem("dq2"), 4 * (j + 1))
            p.inc(g.dma_start(out=w1q_dram[j], in_=dqw[:, :]), "qh", 16)
    block.gpsimd(d1_gB)

    QW_D2A = p.v("qw")

    BS_D2A = p.v("bs")

    def d2_syncA(sync):
        sync.wait_ge(p.sem("dq2"), 32)
        for i in range(NID):
            if i >= 1:
                sync.wait_ge(p.sem("bs"), BS_D2A + i)
            p.inc(sync.dma_start(out=ch1[:, 0:1024],
                                 in_=w2_th[i * 128:(i + 1) * 128, :]),
                  "qw", 16)
    block.sync(d2_syncA)

    def d2_vA(vector):
        for i in range(NID):
            vector.wait_ge(p.sem("qw"), QW_D2A + 16 * (i + 1))
            vector.reduce_sum(sml[:, 20:21], ch1[:, 0:1024], axis=AX,
                              apply_absolute_value=True)
            p.inc(vector.drain(), "bs")
            if i == 0:
                vector.tensor_copy(sml[:, 21:22], sml[:, 20:21])
            else:
                vector.tensor_tensor(sml[:, 21:22], sml[:, 21:22],
                                     sml[:, 20:21], OP.add)
            p.inc(vector.drain(), "bv")
    block.vector(d2_vA)
    BV_W2S = p.v("bv")

    def d2_cell(tensor):
        tensor.wait_ge(p.sem("bv"), BV_W2S)
        p.inc(tensor.matmul(pb[0][0:1, 0:1], sml[:, 21:22], onsq[:, 0:1],
                            start=True, stop=True), "dm")
    block.tensor(d2_cell)
    DM_W2 = p.v("dm")

    def d2_cs(scalar):
        scalar.wait_ge(p.sem("dm"), DM_W2)
        scalar.copy(out=sml[0:1, 52:53], in_=pb[0][0:1, 0:1])
        p.inc(scalar.drain(), "bs")
    block.scalar(d2_cs)
    BS_W2C = p.v("bs")

    def d2_cv(vector):
        vector.wait_ge(p.sem("bs"), BS_W2C)
        vector.tensor_scalar(sml[0:1, 53:54], sml[0:1, 52:53],
                             1.0 / 4194304.0, None, OP.mult)
        vector.drain()
        vector.tensor_scalar(sml[0:1, 54:55], sml[0:1, 53:54],
                             1e-5, None, OP.max)
        vector.drain()
        vector.reciprocal(sml[0:1, 55:56], sml[0:1, 54:55])
        p.inc(vector.drain(), "bv")
    block.vector(d2_cv)
    BV_W2C = p.v("bv")

    def d2_ct(tensor):
        tensor.wait_ge(p.sem("bv"), BV_W2C)
        p.inc(tensor.matmul(pb[1][:, 0:1], onr[:], sml[0:1, 55:56],
                            start=True, stop=True), "dm")
    block.tensor(d2_ct)
    DM_W2R = p.v("dm")

    def d2_cs2(scalar):
        scalar.wait_ge(p.sem("dm"), DM_W2R)
        scalar.copy(out=qsc[:, 1:2], in_=pb[1][:, 0:1])
        p.inc(scalar.drain(), "bs")
    block.scalar(d2_cs2)
    BS_W2R = p.v("bs")

    QW_D2B = p.v("qw")
    QH_D2B = p.v("qh")

    def d2_syncB(sync):
        sync.wait_ge(p.sem("bs"), BS_W2R)
        for i in range(NID):
            if i >= 2:
                sync.wait_ge(p.sem("dq2"), 32 + i - 1)
            p.inc(sync.dma_start(
                out=ch1[:, (i % 2) * 1024:(i % 2) * 1024 + 1024],
                in_=w2_th[i * 128:(i + 1) * 128, :]), "qw", 16)
    block.sync(d2_syncB)

    def d2_vB(vector):
        for i in range(NID):
            vector.wait_ge(p.sem("qw"), QW_D2B + 16 * (i + 1))
            if i >= 2:
                vector.wait_ge(p.sem("qh"), QH_D2B + 16 * (i - 1))
            sl2 = slice((i % 2) * 1024, (i % 2) * 1024 + 1024)
            vector.scalar_tensor_tensor(ich2[:, 0:1024], ch1[:, sl2],
                                        qsc[:, 1:2], ch1[:, sl2],
                                        OP.mult, OP.bypass)
            vector.drain()
            vector.tensor_scalar(dqw[:, (i % 2) * 1024:(i % 2) * 1024 + 1024],
                                 ich2[:, 0:1024], -1.0, 1.0, OP.max, OP.min)
            p.inc(vector.drain(), "dq2")
    block.vector(d2_vB)

    def d2_gB(g):
        for i in range(NID):
            g.wait_ge(p.sem("dq2"), 32 + i + 1)
            p.inc(g.dma_start(out=w2q_dram[i],
                              in_=dqw[:, (i % 2) * 1024:(i % 2) * 1024 + 1024]),
                  "qh", 16)
    block.gpsimd(d2_gB)

    QG_DI = p.v("qg")

    def di_g(g):
        g.wait_ge(p.sem("bv"), BV_XQ)
        g.wait_ge(p.sem("bv"), BV_W1C)
        for tb in range(16):
            p.inc(g.dma_start(out=crep[0:1, tb * 128:(tb + 1) * 128],
                              in_=sml[:, tb:tb + 1]), "qg", 16)
    block.gpsimd(di_g)
    QG_DIR = p.v("qg")

    def di_v(vector):
        vector.wait_ge(p.sem("qg"), QG_DIR)
        vector.tensor_scalar(crep[0:1, 0:M], crep[0:1, 0:M], 1.0 / 127.0,
                             None, OP.mult)
        vector.drain()
        vector.scalar_tensor_tensor(crep[0:1, 0:M], crep[0:1, 0:M],
                                    sml[0:1, 50:51], crep[0:1, 0:M],
                                    OP.mult, OP.bypass)
        p.inc(vector.drain(), "bv")
    block.vector(di_v)
    BV_IR = p.v("bv")

    def di_t(tensor):
        tensor.wait_ge(p.sem("bv"), BV_IR)
        for c in range(NCH):
            p.inc(tensor.matmul(pb[c][:, :], onr[:],
                                crep[0:1, c * 512:c * 512 + 512],
                                start=True, stop=True), "dm")
    block.tensor(di_t)
    DM_IR = p.v("dm")

    def di_s(scalar):
        for c in range(NCH):
            scalar.wait_ge(p.sem("dm"), DM_IR - NCH + c + 1)
            scalar.activation(out=crep[:, c * 512:c * 512 + 512],
                              in_=pb[c][:, :], func=AF.Copy)
            p.inc(scalar.drain(), "dev")
    block.scalar(di_s)
    DEV_IR = p.v("dev")

    QW_G1 = p.v("qw")

    def g1_sync(sync):
        sync.wait_ge(p.sem("dq2"), 64)
        sync.wait_ge(p.sem("qh"), QH_D2B)
        for e in range(NID):
            if e >= 2:
                sync.wait_ge(p.sem("dm"), DM_IR + 4 * (e - 1))
            for j in range(NJ):
                p.inc(sync.dma_start(
                    out=dwe[:, (e % 2) * 1024 + j * 128:
                            (e % 2) * 1024 + j * 128 + 128],
                    in_=w1q_dram[j][:, e * 128:(e + 1) * 128]), "qw", 16)
    block.sync(g1_sync)

    def g1_tensor(tensor):
        tensor.wait_ge(p.sem("dev"), DEV_IR)
        for e in range(NID):
            tensor.wait_ge(p.sem("qw"), QW_G1 + 128 * (e + 1))
            for c in range(NCH):
                gk = e * NCH + c
                if gk >= 4:
                    tensor.wait_ge(p.sem("dev"), DEV_IR + gk - 3)
                for j in range(NJ):
                    mm = tensor.matmul(
                        pb[gk % 4][:, :],
                        dwe[:, (e % 2) * 1024 + j * 128:
                            (e % 2) * 1024 + j * 128 + 128],
                        xqb[:, j * M + c * 512: j * M + c * 512 + 512],
                        start=(j == 0), stop=(j == NJ - 1))
                p.inc(mm, "dm")
    block.tensor(g1_tensor)
    DM_G1 = p.v("dm") - NID * NCH

    QH_G1 = p.v("qh")

    def g1_scalar(scalar):
        for e in range(NID):
            for c in range(NCH):
                gk = e * NCH + c
                scalar.wait_ge(p.sem("dm"), DM_G1 + gk + 1)
                if gk >= 2:
                    scalar.wait_ge(p.sem("dvq"), gk - 1)
                    scalar.wait_ge(p.sem("qh"), QH_G1 + 16 * (gk - 1))
                scalar.activation(out=ch2[:, (gk % 2) * 512:
                                          (gk % 2) * 512 + 512],
                                  in_=pb[gk % 4][:, :], func=AF.Copy)
                p.inc(scalar.drain(), "dev")
    block.scalar(g1_scalar)
    DEV_G1 = p.v("dev") - NID * NCH

    def g1_vector(vector):
        for e in range(NID):
            for c in range(NCH):
                gk = e * NCH + c
                vector.wait_ge(p.sem("dev"), DEV_G1 + gk + 1)
                if gk >= 2:
                    vector.wait_ge(p.sem("qh"), QH_G1 + 16 * (gk - 1))
                sl = slice((gk % 2) * 512, (gk % 2) * 512 + 512)
                vector.tensor_tensor(ch2[:, sl], ch2[:, sl],
                                     crep[:, c * 512:c * 512 + 512], OP.mult)
                vector.drain()
                if c == 0:
                    vector.tensor_copy(hxD1[:, e:e + 1],
                                       ch2[:, (gk % 2) * 512:
                                           (gk % 2) * 512 + 1])
                if c == NCH - 1:
                    vector.tensor_copy(hxD1[:, 32 + e:33 + e],
                                       ch2[:, (gk % 2) * 512 + 511:
                                           (gk % 2) * 512 + 512])
                p.inc(vector.drain(), "dvq")
    block.vector(g1_vector)

    def g1_g(g):
        for e in range(NID):
            for c in range(NCH):
                gk = e * NCH + c
                g.wait_ge(p.sem("dvq"), gk + 1)
                p.inc(g.dma_start(out=hD_dram[e][:, c * 512:c * 512 + 512],
                                  in_=ch2[:, (gk % 2) * 512:
                                          (gk % 2) * 512 + 512]), "qh", 16)
    block.gpsimd(g1_g)

    def d3_v0(vector):
        vector.wait_ge(p.sem("dvq"), NID * NCH)
        vector.scalar_tensor_tensor(hxD1[:, 0:32], hxD1[:, 0:32], mko[:, 0:1],
                                    hxD1[:, 0:32], OP.mult, OP.bypass)
        vector.drain()
        vector.scalar_tensor_tensor(hxD1[:, 32:64], hxD1[:, 32:64],
                                    mke[:, 0:1], hxD1[:, 32:64],
                                    OP.mult, OP.bypass)
        p.inc(vector.drain(), "bv")
    block.vector(d3_v0)
    BV_HXD = p.v("bv")

    def d3_g0(g):
        g.wait_ge(p.sem("bv"), BV_HXD)
        p.inc(g.dma_start(out=hxD_in, in_=hxD1), "qg", 16)
        g.wait_ge(p.sem("qg"), p.v("qg"))
        p.inc(g.collective_compute("AllReduce", OP.add,
                                   replica_groups=[[0, 1], [2, 3], [4, 5],
                                                   [6, 7]],
                                   ins=[hxD_in], outs=[hxD_out]), "cc")
        g.wait_ge(p.sem("cc"), p.v("cc"))
        p.inc(g.dma_start(out=hxD1, in_=hxD_out), "qg", 16)
    block.gpsimd(d3_g0)
    QG_HXD = p.v("qg")

    def d3_v1(vector):
        vector.wait_ge(p.sem("qg"), QG_HXD)
        vector.scalar_tensor_tensor(hxD1[:, 32:64], hxD1[:, 32:64],
                                    mko[:, 0:1], hxD1[:, 32:64],
                                    OP.mult, OP.bypass)
        vector.drain()
        vector.scalar_tensor_tensor(hxD1[:, 0:32], hxD1[:, 0:32], mke[:, 0:1],
                                    hxD1[:, 0:32], OP.mult, OP.bypass)
        p.inc(vector.drain(), "bv")
    block.vector(d3_v1)
    BV_HXE = p.v("bv")

    QW_D3 = p.v("qw")
    QH_D3 = p.v("qh")

    def d3_sync(sync):
        sync.wait_ge(p.sem("bv"), BV_HXE)
        for i in range(NID):
            if i >= 2:
                sync.wait_ge(p.sem("dcv"), i - 1)
            sync.wait_ge(p.sem("qh"), QH_G1 + 16 * 4 * (i + 1))
            ins = ch2[:, 0:M] if i % 2 == 0 else brep[:, 0:M]
            p.inc(sync.dma_start(out=ins, in_=hD_dram[i]), "qw", 16)
    block.sync(d3_sync)

    def runmax(vector, ii):
        vector.wait_ge(p.sem("dsl"), ii + 1)
        o2 = ch1[:, 0:M] if ii % 2 == 0 else crep[:, 0:M]
        vector.tensor_scalar(hbuf[:, 0:M], o2, -1.0, None, OP.mult)
        vector.drain()
        vector.tensor_tensor(hbuf[:, 0:M], hbuf[:, 0:M], o2, OP.max)
        vector.drain()
        if ii == 0:
            vector.tensor_copy(mD[:, 0:M], hbuf[:, 0:M])
        else:
            vector.tensor_tensor(mD[:, 0:M], mD[:, 0:M], hbuf[:, 0:M],
                                 OP.max)
        p.inc(vector.drain(), "dm2")

    def d3_v(vector):
        for i in range(NID):
            vector.wait_ge(p.sem("qw"), QW_D3 + 16 * (i + 1))
            if i >= 2:
                vector.wait_ge(p.sem("qh"), QH_D3 + 16 * (i - 1))
            x_ = ch2[:, 0:M] if i % 2 == 0 else brep[:, 0:M]
            o_ = ch1[:, 0:M] if i % 2 == 0 else crep[:, 0:M]
            vector.scalar_tensor_tensor(o_, x_, cwD[:, 3 * i + 1:3 * i + 2],
                                        x_, OP.mult, OP.bypass)
            vector.drain()
            vector.scalar_tensor_tensor(o_[:, 1:2048], x_[:, 0:2047],
                                        cwD[:, 3 * i:3 * i + 1],
                                        o_[:, 1:2048], OP.mult, OP.add)
            vector.drain()
            vector.scalar_tensor_tensor(o_[:, 0:2047], x_[:, 1:2048],
                                        cwD[:, 3 * i + 2:3 * i + 3],
                                        o_[:, 0:2047], OP.mult, OP.add)
            vector.drain()
            vector.scalar_tensor_tensor(o_[:, 0:1], hxD1[:, 32 + i:33 + i],
                                        cwD[:, 3 * i:3 * i + 1],
                                        o_[:, 0:1], OP.mult, OP.add)
            vector.scalar_tensor_tensor(o_[:, 2047:2048], hxD1[:, i:i + 1],
                                        cwD[:, 3 * i + 2:3 * i + 3],
                                        o_[:, 2047:2048], OP.mult, OP.add)
            vector.drain()
            vector.scalar_tensor_tensor(o_, o_, cbD[:, i:i + 1], o_,
                                        OP.add, OP.bypass)
            p.inc(vector.drain(), "dcv")
            if i >= 1:
                runmax(vector, i - 1)
        runmax(vector, NID - 1)
    block.vector(d3_v)

    def d3_s(scalar):
        for i in range(NID):
            scalar.wait_ge(p.sem("dcv"), i + 1)
            o_ = ch1[:, 0:M] if i % 2 == 0 else crep[:, 0:M]
            scalar.activation(out=o_, in_=o_, func=AF.Silu)
            p.inc(scalar.drain(), "dsl")
    block.scalar(d3_s)

    def d3_g(g):
        for i in range(NID):
            g.wait_ge(p.sem("dm2"), i + 1)
            o_ = ch1[:, 0:M] if i % 2 == 0 else crep[:, 0:M]
            p.inc(g.dma_start(out=hD_dram[i], in_=o_), "qh", 16)
    block.gpsimd(d3_g)

    DTR_S2 = p.v("dtr")
    BV_S2A = p.v("bv")

    def d3_t2(tensor):
        tensor.wait_ge(p.sem("dm2"), NID)
        for tb in range(16):
            if tb >= 4:
                tensor.wait_ge(p.sem("bv"), BV_S2A + tb - 3)
            p.inc(tensor.transpose(pb[4 + tb % 4][:, 0:128],
                                   mD[:, tb * 128:(tb + 1) * 128],
                                   idf[:, :]), "dtr")
    block.tensor(d3_t2)

    def d3_v5(vector):
        for tb in range(16):
            vector.wait_ge(p.sem("dtr"), DTR_S2 + tb + 1)
            vector.reduce_max(sml[:, 32 + tb:33 + tb],
                              pb[4 + tb % 4][:, 0:128], axis=AX)
            p.inc(vector.drain(), "bv")
        vector.tensor_scalar(sml[:, 32:48], sml[:, 32:48], 1e-5, None, OP.max)
        p.inc(vector.drain(), "bv")
    block.vector(d3_v5)
    BV_S2C = p.v("bv")

    def d3_g2(g):
        g.wait_ge(p.sem("bv"), BV_S2C)
        g.wait_ge(p.sem("qh"), QH_D3 + 16 * NID)
        for tb in range(16):
            p.inc(g.dma_start(out=crep[0:1, tb * 128:(tb + 1) * 128],
                              in_=sml[:, 32 + tb:33 + tb]), "qg", 16)
    block.gpsimd(d3_g2)
    QG_S2R = p.v("qg")

    def d3_v6(vector):
        vector.wait_ge(p.sem("qg"), QG_S2R)
        vector.reciprocal(crep[0:1, 0:M], crep[0:1, 0:M])
        vector.drain()
        vector.tensor_scalar(crep[0:1, 0:M], crep[0:1, 0:M], 127.0, None,
                             OP.mult)
        p.inc(vector.drain(), "bv")
    block.vector(d3_v6)
    BV_S2R = p.v("bv")

    def d3_t3(tensor):
        tensor.wait_ge(p.sem("bv"), BV_S2R)
        for c in range(NCH):
            p.inc(tensor.matmul(pb[c][:, :], onr[:],
                                crep[0:1, c * 512:c * 512 + 512],
                                start=True, stop=True), "dm")
    block.tensor(d3_t3)
    DM_S2 = p.v("dm")

    def d3_s2(scalar):
        for c in range(NCH):
            scalar.wait_ge(p.sem("dm"), DM_S2 - NCH + c + 1)
            scalar.activation(out=crep[:, c * 512:c * 512 + 512],
                              in_=pb[c][:, :], func=AF.Copy)
            p.inc(scalar.drain(), "dev")
    block.scalar(d3_s2)
    DEV_S2 = p.v("dev")

    QW_D4 = p.v("qw")
    QH_D4 = p.v("qh")

    def d4_sync(sync):
        sync.wait_ge(p.sem("dev"), DEV_S2)
        for i in range(NID):
            if i >= 2:
                sync.wait_ge(p.sem("dq3"), i - 1)
            sync.wait_ge(p.sem("qh"), QH_D3 + 16 * (i + 1))
            ins = ch2[:, 0:M] if i % 2 == 0 else brep[:, 0:M]
            p.inc(sync.dma_start(out=ins, in_=hD_dram[i]), "qw", 16)
    block.sync(d4_sync)

    def d4_v(vector):
        for i in range(NID):
            vector.wait_ge(p.sem("qw"), QW_D4 + 16 * (i + 1))
            if i >= 2:
                vector.wait_ge(p.sem("qh"), QH_D4 + 16 * (i - 1))
            ins = ch2[:, 0:M] if i % 2 == 0 else brep[:, 0:M]
            vector.tensor_tensor(ich1[:, :], ins, crep[:, 0:M], OP.mult)
            vector.drain()
            vector.tensor_scalar(dqw[:, (i % 2) * 2048:(i % 2) * 2048 + 2048],
                                 ich1[:, :], -128.0, 127.0, OP.max, OP.min)
            p.inc(vector.drain(), "dq3")
    block.vector(d4_v)

    def d4_g(g):
        for i in range(NID):
            g.wait_ge(p.sem("dq3"), i + 1)
            p.inc(g.dma_start(out=h2q_dram[i],
                              in_=dqw[:, (i % 2) * 2048:(i % 2) * 2048 + 2048]),
                  "qh", 16)
    block.gpsimd(d4_g)

    QG_I2 = p.v("qg")

    def d4_g2(g):
        g.wait_ge(p.sem("dq3"), NID)
        g.wait_ge(p.sem("bv"), BV_W2C)
        for tb in range(16):
            p.inc(g.dma_start(out=crep[0:1, tb * 128:(tb + 1) * 128],
                              in_=sml[:, 32 + tb:33 + tb]), "qg", 16)
    block.gpsimd(d4_g2)
    QG_I2R = p.v("qg")

    def d4_v2(vector):
        vector.wait_ge(p.sem("qg"), QG_I2R)
        vector.tensor_scalar(crep[0:1, 0:M], crep[0:1, 0:M], 1.0 / 127.0,
                             None, OP.mult)
        vector.drain()
        vector.scalar_tensor_tensor(crep[0:1, 0:M], crep[0:1, 0:M],
                                    sml[0:1, 54:55], crep[0:1, 0:M],
                                    OP.mult, OP.bypass)
        p.inc(vector.drain(), "bv")
    block.vector(d4_v2)
    BV_I2 = p.v("bv")

    def d4_t(tensor):
        tensor.wait_ge(p.sem("bv"), BV_I2)
        for c in range(NCH):
            p.inc(tensor.matmul(pb[c][:, :], onr[:],
                                crep[0:1, c * 512:c * 512 + 512],
                                start=True, stop=True), "dm")
    block.tensor(d4_t)
    DM_I2 = p.v("dm")

    def d4_s(scalar):
        for c in range(NCH):
            scalar.wait_ge(p.sem("dm"), DM_I2 - NCH + c + 1)
            scalar.activation(out=crep[:, c * 512:c * 512 + 512],
                              in_=pb[c][:, :], func=AF.Copy)
            p.inc(scalar.drain(), "dev")
    block.scalar(d4_s)
    DEV_I2 = p.v("dev")

    QW_G2 = p.v("qw")

    def d5_sync(sync):
        sync.wait_ge(p.sem("dev"), DEV_I2)
        sync.wait_ge(p.sem("qh"), QH_D4 + 16 * NID)
        for c in range(NCH):
            for i in range(NID):
                k5 = c * NID + i
                if k5 >= 2:
                    sync.wait_ge(p.sem("dm5"), k5 - 1)
                p.inc(sync.dma_start(
                    out=dwe[:, (k5 % 2) * 1024:(k5 % 2) * 1024 + 1024],
                    in_=w2q_dram[i]), "qw", 16)
                p.inc(sync.dma_start(
                    out=dqx[:, (k5 % 2) * 512:(k5 % 2) * 512 + 512],
                    in_=h2q_dram[i][:, c * 512:c * 512 + 512]), "qw", 16)
    block.sync(d5_sync)

    def d5_tensor(tensor):
        tensor.wait_ge(p.sem("dev"), DEV_I2)
        for c in range(NCH):
            if c >= 1:
                tensor.wait_ge(p.sem("dde"), c * NJ)
            for i in range(NID):
                k5 = c * NID + i
                tensor.wait_ge(p.sem("qw"), QW_G2 + 32 * (k5 + 1))
                for e in range(NJ):
                    mm = tensor.matmul(
                        pb[e][:, :],
                        dwe[:, (k5 % 2) * 1024 + e * 128:
                            (k5 % 2) * 1024 + e * 128 + 128],
                        dqx[:, (k5 % 2) * 512:(k5 % 2) * 512 + 512],
                        start=(i == 0), stop=(i == NID - 1))
                p.inc(mm, "dm5")
    block.tensor(d5_tensor)

    def d5_scalar(scalar):
        for c in range(NCH):
            scalar.wait_ge(p.sem("dm5"), (c + 1) * NID)
            for e in range(NJ):
                k6 = c * NJ + e
                if k6 >= 2:
                    scalar.wait_ge(p.sem("dde"), k6 - 1)
                scalar.activation(out=ch1[:, (k6 % 2) * 512:
                                          (k6 % 2) * 512 + 512],
                                  in_=pb[e][:, :], func=AF.Copy)
                p.inc(scalar.drain(), "de2")
    block.scalar(d5_scalar)

    def d5_vector(vector):
        for c in range(NCH):
            for e in range(NJ):
                k6 = c * NJ + e
                vector.wait_ge(p.sem("de2"), k6 + 1)
                sl = slice((k6 % 2) * 512, (k6 % 2) * 512 + 512)
                vector.tensor_tensor(ch1[:, sl], ch1[:, sl],
                                     crep[:, c * 512:c * 512 + 512], OP.mult)
                vector.drain()
                vector.tensor_tensor(xsl(e, c), xsl(e, c), ch1[:, sl], OP.add)
                p.inc(vector.drain(), "dde")
    block.vector(d5_vector)

    nmD = emit_ln("lD", pre=[("dde", NCH * NJ)])

    if DEBUG_DUMP == "bw0":
        def dbg_g(g):
            g.wait_ge(p.sem("bts"), 16)
            g.wait_ge(p.sem("bq"), NJ)
            p.inc(g.dma_start(out=out_t[0:128, 0:2048], in_=bqw[:, 0:2048]),
                  "qout", 16)
            p.inc(g.dma_start(out=out_t[128:256, 0:2048], in_=xqb[:, 0:2048]),
                  "qout", 16)
            p.inc(g.dma_start(out=out_t[256:384, 0:2048],
                              in_=xqb[:, 2048:4096]), "qout", 16)
            g.wait_ge(p.sem("qout"), p.v("qout"))
        block.gpsimd(dbg_g)
    elif DEBUG_DUMP == "bscale":
        def dbg_sync(sync):
            sync.wait_ge(p.sem("bs"), BS_Q)
            p.inc(sync.dma_start(out=out_t[0:128, 0:1344], in_=pack1[:, :]), "qout", 16)
            p.inc(sync.dma_start(out=out_t[128:256, 0:16], in_=qsc[:, :]), "qout", 16)
            p.inc(sync.dma_start(out=out_t[256:384, 0:64], in_=sml[:, :]), "qout", 16)
            sync.wait_ge(p.sem("qout"), p.v("qout"))
        block.sync(dbg_sync)
    elif DEBUG_DUMP == "bh":
        def dbg_sync(sync):
            sync.wait_ge(p.sem("qh"), QH_SP)
            for jj in range(8):
                p.inc(sync.dma_start(
                    out=out_t[jj * 128:(jj + 1) * 128, :], in_=h_dram[jj]),
                    "qout", 16)
            sync.wait_ge(p.sem("qout"), p.v("qout"))
        block.sync(dbg_sync)
    elif DEBUG_DUMP == "bhq":
        def dbg_g(g):
            g.wait_ge(p.sem("qhq"), QHQ)
            for jj in range(8):
                p.inc(g.dma_start(out=out_t[jj * 128:(jj + 1) * 128, :],
                                  in_=hq_dram[DEBUG_BASE + jj]), "qout", 16)
            g.wait_ge(p.sem("qout"), p.v("qout"))
        block.gpsimd(dbg_g)
    elif DEBUG_DUMP == "bwq":
        def dbg_g(g):
            g.wait_ge(p.sem("qwd"), QWD)
            for jj in range(8):
                p.inc(g.dma_start(out=out_t[jj * 128:(jj + 1) * 128, 0:1024],
                                  in_=wdq_dram[DEBUG_BASE + jj]), "qout", 16)
            g.wait_ge(p.sem("qout"), p.v("qout"))
        block.gpsimd(dbg_g)
    elif DEBUG_DUMP == "bst":
        def dbg_sync(sync):
            sync.wait_ge(p.sem("bde"), NCH * NJ)
            p.inc(sync.dma_start(out=out_t[0:1, :], in_=aux[0:1, 0:2048]),
                  "qout", 16)
            p.inc(sync.dma_start(out=out_t[1:2, :], in_=aux[0:1, 2048:4096]),
                  "qout", 16)
            sync.wait_ge(p.sem("qout"), p.v("qout"))
        block.sync(dbg_sync)
    elif DEBUG_DUMP in ("bxw", "bx6", "bx7", "bx8", "bxy", "bxm", "bxl"):
        pass
    elif DEBUG_DUMP == "bx1":
        def dbg_g(g):
            g.wait_ge(p.sem("bhq"), NI)
            p.inc(g.dma_start(out=out_t.rearrange("(j p) t -> p j t", p=128),
                              in_=xs.rearrange("p (j t) -> p j t", j=NJ)),
                  "qout", 16)
            g.wait_ge(p.sem("qout"), p.v("qout"))
        block.gpsimd(dbg_g)
    elif DEBUG_DUMP == "bdown":
        def dbg_sync(sync):
            sync.wait_ge(p.sem("bde"), NCH * NJ)
            p.inc(sync.dma_start(out=out_t.rearrange("(j p) t -> p j t", p=128),
                                 in_=aux.rearrange("p (j t) -> p j t", j=NJ)),
                  "qout", 16)
            sync.wait_ge(p.sem("qout"), p.v("qout"))
        block.sync(dbg_sync)
    elif DEBUG_DUMP == "bd":
        def dbg_sync(sync):
            sync.wait_ge(p.sem("bde"), NCH * NJ)
            p.inc(sync.dma_start(out=out_t.rearrange("(j p) t -> p j t", p=128),
                                 in_=xs.rearrange("p (j t) -> p j t", j=NJ)),
                  "qout", 16)
            sync.wait_ge(p.sem("qout"), p.v("qout"))
        block.sync(dbg_sync)
    elif DEBUG_DUMP == "fr":
        def dbg_v(vector):
            vector.wait_ge(p.sem("sa"), NJ * NCH)
            vector.wait_ge(p.sem("vl"), NJ)
            vector.tensor_copy(aux[0:1, 0:8], lrow)
            p.inc(vector.tensor_copy(aux[0:1, 8:16], lrow), "dbg")
        block.vector(dbg_v)

        def dbg_sync(sync):
            sync.wait_ge(p.sem("dbg"), 1)
            p.inc(sync.dma_start(out=out_t.rearrange("(j p) t -> p j t", p=128),
                                 in_=aux.rearrange("p (j t) -> p j t", j=NJ)),
                  "qout", 16)
            sync.wait_ge(p.sem("qout"), p.v("qout"))
        block.sync(dbg_sync)
    elif DEBUG_DUMP == "y":
        def dbg_v(vector):
            vector.wait_ge(p.sem("sv"), p.v("sv"))
            vector.tensor_copy(aux[:, 0:8], ACC)
            vector.tensor_copy(aux[:, 8:16], ACCA)
            vector.tensor_copy(aux[:, 16:24], ACCB)
            vector.tensor_copy(aux[:, 24:32], CTX)
            vector.wait_ge(p.sem("sfh"), NJ)
            vector.tensor_copy(aux[:, 32:40], FH)
            vector.tensor_copy(aux[0:1, 1 * M: 1 * M + M], rwA)
            vector.tensor_copy(aux[0:1, 2 * M: 2 * M + M], rwB)
            vector.tensor_copy(aux[0:1, 48:56], lrow)

            p.inc(vector.tensor_copy(aux[0:1, 40:48], cA_r), "dbg")
        block.vector(dbg_v)

        def dbg_sync(sync):
            sync.wait_ge(p.sem("dbg"), 1)
            p.inc(sync.dma_start(out=out_t.rearrange("(j p) t -> p j t", p=128),
                                 in_=aux.rearrange("p (j t) -> p j t", j=NJ)),
                  "qout", 16)
            sync.wait_ge(p.sem("qout"), p.v("qout"))
        block.sync(dbg_sync)
    else:
        def out_sync(sync):
            nmF = {1: nmB, 2: nmB2, 3: nmC}.get(DEBUG_UPTO, nmD)
            sync.wait_ge(p.sem(nmF[0]), nmF[1])
            p.inc(sync.dma_start(out=out_t.rearrange("(j p) t -> p j t", p=128),
                                 in_=xs.rearrange("p (j t) -> p j t", j=NJ)),
                  "qout", 16)
            sync.wait_ge(p.sem("qout"), p.v("qout"))
        block.sync(out_sync)

    blk.__exit__(None, None, None)
    return p


_CACHED = {}


def kernel(**inputs):
    h = np.asarray(inputs["L0_attn_h"], dtype=np.float32)
    Wr = np.asarray(inputs["L0_attn_Wr"], dtype=np.float32)
    Wf = np.asarray(inputs["L0_attn_Wf"], dtype=np.float32)
    att_r = np.asarray(inputs["L0_attn_att_r"], dtype=np.float32)
    g0 = np.asarray(inputs["L0_attn_ln_gamma"], dtype=np.float32)
    b0 = np.asarray(inputs["L0_attn_ln_beta"], dtype=np.float32)
    import ml_dtypes
    wg_n = np.ascontiguousarray(np.asarray(inputs["L0_ffn_w_gate"], np.float32))
    wu_n = np.ascontiguousarray(np.asarray(inputs["L0_ffn_w_up"], np.float32))
    wd_n = np.ascontiguousarray(np.asarray(inputs["L0_ffn_w_down"], np.float32))
    clipx = np.array([[float(np.asarray(inputs["L0_ffn_clip_x"])[0]),
                       float(np.asarray(inputs["L0_ffn_clip_down"])[0])]],
                     np.float32)
    identb = np.eye(128, dtype=ml_dtypes.bfloat16)

    wr_t = np.ascontiguousarray(Wr.T)
    wf_t = np.ascontiguousarray(Wf.T)
    w_in_t = np.ascontiguousarray(np.asarray(inputs["L1_attn_W_in"], np.float32).T)
    w_dt_t = np.ascontiguousarray(np.asarray(inputs["L1_attn_W_dt"], np.float32).T)
    w_x_t = np.ascontiguousarray(np.asarray(inputs["L1_attn_W_x"], np.float32).T)
    w_out_t = np.ascontiguousarray(np.asarray(inputs["L1_attn_W_out"], np.float32).T)
    a_log = np.asarray(inputs["L1_attn_A_log"], np.float32)
    a_log_c = np.ascontiguousarray(
        a_log.reshape(8, 128, 16).transpose(1, 0, 2).reshape(128, 128))
    d_colh = np.ascontiguousarray(
        np.asarray(inputs["L1_attn_D_param"], np.float32).reshape(8, 128).T)
    bdt_colh = np.ascontiguousarray(
        np.asarray(inputs["L1_attn_b_dt"], np.float32).reshape(8, 128).T)
    cw_colh = np.ascontiguousarray(
        np.asarray(inputs["L1_attn_conv_w"], np.float32)[:, 0, :]
        .reshape(8, 128, 3).transpose(1, 0, 2).reshape(128, 24))
    g3ch = np.ascontiguousarray(
        np.asarray(inputs["L1_attn_gamma"], np.float32).reshape(8, 128).T)
    b3ch = np.ascontiguousarray(
        np.asarray(inputs["L1_attn_beta"], np.float32).reshape(8, 128).T)
    w1_th = np.ascontiguousarray(np.asarray(inputs["L1_ffn_w1"], np.float32).T)
    w2_th = np.ascontiguousarray(np.asarray(inputs["L1_ffn_w2"], np.float32).T)
    cwD_h = np.ascontiguousarray(
        np.asarray(inputs["L1_ffn_conv_w"], np.float32)[:, 0, :]
        .reshape(32, 128, 3).transpose(1, 0, 2).reshape(128, 96))
    cbD_h = np.ascontiguousarray(
        np.asarray(inputs["L1_ffn_conv_b"], np.float32).reshape(32, 128).T)
    idf_h = np.eye(128, dtype=np.float32)
    attr_c = np.ascontiguousarray(att_r.reshape(128, 1))
    g0c = np.ascontiguousarray(g0.reshape(NJ, 128).T)
    b0c = np.ascontiguousarray(b0.reshape(NJ, 128).T)
    ones_r = np.ones((1, 128), dtype=np.float32)
    ones_sq = np.ones((128, 128), dtype=np.float32)

    prog = build()
    in_maps = []
    for c in range(8):
        b, t0 = c // 2, (c % 2) * M
        im = {
            "x_in": np.ascontiguousarray(h[b, t0:t0 + M, :].T),
            "wr_t": wr_t, "wf_t": wf_t, "attr_c": attr_c,
            "g0c": g0c, "b0c": b0c,
            "ones_r": ones_r, "ones_sq": ones_sq,
            "wg_n": wg_n, "wu_n": wu_n, "wd_n": wd_n, "clipx": clipx,
            "identb": identb,
            "w_in_t": w_in_t, "w_dt_t": w_dt_t, "w_x_t": w_x_t,
            "w_out_t": w_out_t, "a_log_c": a_log_c, "d_colh": d_colh,
            "bdt_colh": bdt_colh, "cw_colh": cw_colh,
            "g3ch": g3ch, "b3ch": b3ch,
            "w1_th": w1_th, "w2_th": w2_th, "cwD_h": cwD_h,
            "cbD_h": cbD_h, "idf_h": idf_h,
            "mask_e": np.full((128, 1), 1.0 if c % 2 == 0 else 0.0, np.float32),
            "mask_o": np.full((128, 1), 1.0 if c % 2 == 1 else 0.0, np.float32),
            "constc": np.tile(np.array([[1e-5, -0.5, -10.0, 2.0, 1.0, 0.0]], np.float32), (128, 1)),
        }
        in_maps.append(im)

    res = run_bass_kernel_spmd(prog.nc, in_maps, core_ids=list(range(8)))
    out = np.empty((4, 4096, D), dtype=np.float32)
    for c in range(8):
        b, t0 = c // 2, (c % 2) * M
        out[b, t0:t0 + M, :] = res.results[c]["out_t"].T
    return out
